# version 05
# baseline (speedup 1.0000x reference)
import os
import numpy as np
import ml_dtypes
import concourse.bass as bass
import concourse.mybir as mybir
from concourse.bass_utils import run_bass_kernel_spmd

F32 = mybir.dt.float32
BF16 = mybir.dt.bfloat16
I32 = mybir.dt.int32
AF = mybir.ActivationFunctionType
ALU = mybir.AluOpType
P = 128

FULL_CFG = dict(B=2, SEQ=4096, DM=2048, HID=2048, HEADS=4, DF=8192,
                LN_EPS=1e-5, RMS_EPS=1e-6, NITER_BIS=36, debug=False)


class KB:

    def __init__(self):
        nc = bass.Bass("TRN2", target_bir_lowering=False,
                       detect_race_conditions=bool(os.environ.get('KRACE')),
                       dynamic_dma_scratch_size=8192)
        self.nc = nc
        from contextlib import ExitStack
        self.es = ExitStack()
        self.ops = {"t": [], "v": [], "s": [], "g": [], "y": []}
        self.sem = {}
        self.cnt = {}

    def semaphore(self, name):
        self.sem[name] = self.es.enter_context(self.nc.semaphore(name))
        self.cnt[name] = 0

    def t(self, fn):
        self.ops["t"].append(fn)

    def v(self, fn):
        self.ops["v"].append(fn)

    def s(self, fn):
        self.ops["s"].append(fn)

    def g(self, fn):
        self.ops["g"].append(fn)

    def y(self, fn):
        self.ops["y"].append(fn)

    def inc(self, name, k=1):
        self.cnt[name] += k
        return self.sem[name], self.cnt[name]

    def val(self, name):
        return self.cnt[name]

    def wait(self, eng_key, name, value):
        if value <= 0:
            return
        sem = self.sem[name]
        self.ops[eng_key].append(lambda e, s=sem, vv=value: e.wait_ge(s, vv))

    def emit(self):
        nc = self.nc
        with nc.Block() as block:
            def run(lst):
                def f(eng):
                    for fn in lst:
                        fn(eng)
                return f
            block.tensor(run(self.ops["t"]))
            block.vector(run(self.ops["v"]))
            block.scalar(run(self.ops["s"]))
            block.gpsimd(run(self.ops["g"]))
            block.sync(run(self.ops["y"]))
        self.es.close()
        return nc


def build(cfg):
    B, SEQ, DM, HID, HEADS, DF = (cfg[k] for k in
                                  ("B", "SEQ", "DM", "HID", "HEADS", "DF"))
    LN_EPS, RMS_EPS = cfg["LN_EPS"], cfg["RMS_EPS"]
    NITER = cfg["NITER_BIS"]
    TOKC = SEQ // 4
    NTC = TOKC // P
    NTH = NTC // 2
    TOKH = TOKC // 2
    NMM = TOKH
    DMC, HIDC, DFC = DM // P, HID // P, DF // P
    HD = HID // HEADS
    HDC = HD // P
    GC = 4 * HID // P
    KSEL = SEQ // 2
    NQ = 4
    DFQ = DF // NQ
    DFQC = DFQ // P
    MG = min(512, HID)
    assert HID % MG == 0 and DM % MG == 0 and HD % P == 0 and DFQ % MG == 0
    assert NMM <= 512 and TOKC % P == 0 and NTC % 2 == 0

    kb = KB()
    nc = kb.nc

    dx = nc.dram_tensor("x_own", [TOKC, DM], F32, kind="ExternalInput").ap()
    dwin = nc.dram_tensor("winT", [DM, HID], F32, kind="ExternalInput").ap()
    dwg = nc.dram_tensor("wgT", [HID, 4 * HID], F32, kind="ExternalInput").ap()
    dwout = nc.dram_tensor("woutT", [HID, DM], F32, kind="ExternalInput").ap()
    dw1 = nc.dram_tensor("w1", [DM, DF], F32, kind="ExternalInput").ap()
    dw2 = nc.dram_tensor("w2", [DF, DM], F32, kind="ExternalInput").ap()
    dbgt = nc.dram_tensor("bg", [P, GC], F32, kind="ExternalInput").ap()
    db1 = nc.dram_tensor("b1c", [P, DFC], F32, kind="ExternalInput").ap()
    db2 = nc.dram_tensor("b2c", [P, DMC], F32, kind="ExternalInput").ap()
    drms = nc.dram_tensor("rmsc", [P, HDC], F32, kind="ExternalInput").ap()
    dwr = nc.dram_tensor("wrb", [P, DM], F32, kind="ExternalInput").ap()
    dbr = nc.dram_tensor("brb", [P, 1], F32, kind="ExternalInput").ap()
    dmq = nc.dram_tensor("mq", [P, 3], F32, kind="ExternalInput").ap()
    didf = nc.dram_tensor("identf", [P, P], F32, kind="ExternalInput").ap()
    dones = nc.dram_tensor("onespp", [P, P], F32, kind="ExternalInput").ap()
    deps = nc.dram_tensor("epsc", [P, 2], F32, kind="ExternalInput").ap()
    dout = nc.dram_tensor("out", [TOKC, DM], F32, kind="ExternalOutput").ap()

    xn_hbm = nc.dram_tensor("xn_hbm", [P, NTC * DM], BF16)
    h_hbm = nc.dram_tensor("h_hbm", [P, HIDC * TOKC], BF16,
                           kind="ExternalOutput" if os.environ.get("KHOUT") else "Internal")
    s_hbm = nc.dram_tensor("s_hbm", [P, DMC * TOKC], BF16)
    sc_in = nc.dram_tensor("sc_in", [P, NTC], F32)
    sc_out = nc.dram_tensor("sc_out", [4 * P, NTC], F32)
    ag_in = [nc.dram_tensor(f"agi{p}", [P, 2 * HDC], F32) for p in range(HEADS)]
    ag_out = [nc.dram_tensor(f"ago{p}", [4 * P, 2 * HDC], F32)
              for p in range(HEADS)]

    es = kb.es
    XT = 0
    XSCR = NTC * DM
    XST = XSCR + DM
    X_END = XST + 8 * NTC
    RF = 0
    RG = HDC * TOKC
    RIO = 2 * HDC * TOKC
    RC = 3 * HDC * TOKC
    RSA = 4 * HDC * TOKC
    RSB = RSA + TOKC
    RZ = RSB + TOKC
    RU = RZ + TOKC
    RTH = RU + NMM
    RSD = RTH + NMM
    RRS = RSD + NMM
    RCSQ = RU
    R_END = RRS + 2 * NMM
    FO = 0
    FU = DMC * NMM
    FGS = FU + NMM
    FTP = FGS + NMM
    FXJ = FTP + 2 * P
    FOJ = FXJ + 2 * DM
    F_END = FOJ + 2 * DM
    WRB = max(X_END, F_END)
    WRB_END = WRB + DM
    MAIN_END = max(WRB_END, R_END)
    PT = MAIN_END
    SC_ = PT + 0
    MSK = PT + 32
    BLO, BHI, BMID = PT + 64, PT + 65, PT + 66
    BD1, BD2, BCNT = PT + 71, PT + 72, PT + 73
    T1_, T2_, T3_ = PT + 68, PT + 69, PT + 70
    AGS = PT + 96
    AGT = PT + 128
    CMPS = PT + 160
    RCNT = PT + 192
    CIN = PT + 196
    SUMM = PT + 208
    STG = PT + 224
    N32 = STG + 1024
    assert NTC <= 32 and 4 * NTC <= 32 and 2 * HDC <= 16
    a32 = es.enter_context(nc.sbuf_tensor("a32", [P, N32], F32))

    def v32(off, cols):
        return a32[:, off:off + cols]

    XN = 0
    XP = XN + NTC * DM
    H1 = XP
    WS0 = XP + HIDC * TOKC
    SLOTW = max(HIDC * MG, DFQC * P)
    WS1 = WS0 + SLOTW
    HS = WS1 + SLOTW
    HSZ = max(HDC * TOKC, (MG // P) * TOKC, DMC * NMM)
    SSL = HS + HSZ
    N16 = SSL + 2 * TOKH + 32
    a16 = es.enter_context(nc.sbuf_tensor("a16", [P, N16], BF16))

    def v16(off, cols):
        return a16[:, off:off + cols]

    pred = es.enter_context(nc.sbuf_tensor("pred", [P, 1], I32))
    cbr = es.enter_context(nc.sbuf_tensor("cbr", [P, 1], F32))
    cbg = es.enter_context(nc.sbuf_tensor("cbg", [P, GC], F32))
    cbgh = es.enter_context(nc.sbuf_tensor("cbgh", [P, GC], F32))
    cb1 = es.enter_context(nc.sbuf_tensor("cb1", [P, DFC], F32))
    cb2 = es.enter_context(nc.sbuf_tensor("cb2", [P, DMC], F32))
    crms = es.enter_context(nc.sbuf_tensor("crms", [P, HDC], F32))
    cmq = es.enter_context(nc.sbuf_tensor("cmq", [P, 3], F32))
    cidf = es.enter_context(nc.sbuf_tensor("cidf", [P, P], F32))
    ceps = es.enter_context(nc.sbuf_tensor("ceps", [P, 2], F32))
    cones = es.enter_context(nc.sbuf_tensor("cones", [P, P], F32))

    ps_a = es.enter_context(nc.psum_tensor("ps_a", [P, 1024], F32))
    ps_b = es.enter_context(nc.psum_tensor("ps_b", [P, 1024], F32))
    ps_c1 = es.enter_context(nc.psum_tensor("ps_c1", [P, 512], F32))
    ps_m = es.enter_context(nc.psum_tensor("ps_m", [P, 512], F32))
    ps_c2 = es.enter_context(nc.psum_tensor("ps_c2", [P, 1024], F32))

    for nm in ("ld", "w0", "w1s", "ccm", "xnw", "hw", "sw", "xnr", "hr",
               "ssl", "xr", "outw", "ten", "vec", "sca", "gp", "cc", "bar"):
        kb.semaphore(nm)

    BAR_LEVEL = cfg.get("barriers", 0)

    def hard_sync(level=1):
        if BAR_LEVEL < level:
            return
        for ek in ("t", "v", "s", "g"):
            h, _ = kb.inc("bar")
            if ek == "t":
                kb.t(lambda e, hh=h: e.nop(nofuse=True).then_inc(hh, 1))
            elif ek == "v":
                kb.v(lambda e, hh=h: e.engine_nop().then_inc(hh, 1))
            elif ek == "s":
                kb.s(lambda e, hh=h: e.nop(nofuse=True).then_inc(hh, 1))
            else:
                kb.g(lambda e, hh=h: e.nop(nofuse=True).then_inc(hh, 1))
        tgt = kb.val("bar")
        for ek in ("t", "v", "s", "g"):
            kb.wait(ek, "bar", tgt)

    def vd():
        kb.v(lambda e: e.drain())

    def gd():
        kb.g(lambda e: e.drain())

    def sd():
        kb.s(lambda e: e.drain())

    lo = v32(BLO, 1)
    hi = v32(BHI, 1)
    mid = v32(BMID, 1)
    rcnt = v32(RCNT, 1)
    cmps = v32(CMPS, 4 * NTC)
    bis = {"i": 0, "last_ten": 0, "inited": False}

    def drip_bis(n=1):
        for _ in range(n):
            i = bis["i"]
            if i >= NITER:
                return
            bis["i"] += 1
            if not bis["inited"]:
                bis["inited"] = True
                kb.wait("v", "ccm", CCM_SC2)
                kb.v(lambda e: e.memset(lo[:, :], -10000.0))
                kb.v(lambda e: e.memset(hi[:, :], 10000.0))
                vd()
            if i > 0:
                kb.wait("v", "ten", bis["last_ten"])
                kb.v(lambda e: e.tensor_scalar(
                    out=v32(BD1, 1), in0=ps_m[:, 0:1], scalar1=float(KSEL),
                    scalar2=None, op0=ALU.is_ge))
                kb.v(lambda e: e.tensor_tensor(out=v32(BD2, 1), in0=mid[:, :],
                                               in1=lo[:, :], op=ALU.subtract))
                vd()
                kb.v(lambda e: e.scalar_tensor_tensor(
                    out=lo[:, :], in0=v32(BD2, 1), scalar=v32(BD1, 1)[:, 0:1],
                    in1=lo[:, :], op0=ALU.mult, op1=ALU.add))
                vd()
                kb.v(lambda e: e.tensor_scalar(
                    out=v32(BD1, 1), in0=ps_m[:, 0:1], scalar1=float(KSEL),
                    scalar2=None, op0=ALU.is_lt))
                kb.v(lambda e: e.tensor_tensor(out=v32(BD2, 1), in0=mid[:, :],
                                               in1=hi[:, :], op=ALU.subtract))
                vd()
                kb.v(lambda e: e.scalar_tensor_tensor(
                    out=hi[:, :], in0=v32(BD2, 1), scalar=v32(BD1, 1)[:, 0:1],
                    in1=hi[:, :], op0=ALU.mult, op1=ALU.add))
                vd()

            kb.v(lambda e: e.tensor_tensor(out=mid[:, :], in0=lo[:, :],
                                           in1=hi[:, :], op=ALU.add))
            vd()
            kb.v(lambda e: e.tensor_scalar(out=mid[:, :], in0=mid[:, :],
                                           scalar1=0.5, scalar2=None,
                                           op0=ALU.mult))
            vd()
            h, _ = kb.inc("vec")
            kb.v(lambda e, hh=h: e.tensor_scalar(
                out=cmps, in0=v32(AGS, 4 * NTC), scalar1=mid[:, 0:1],
                scalar2=None, op0=ALU.is_ge, op1=ALU.add,
                accum_out=rcnt).then_inc(hh, 1))
            kb.wait("t", "vec", kb.val("vec"))
            h2, _ = kb.inc("ten")
            kb.t(lambda e, hh=h2: e.matmul(ps_m[:, 0:1], cones[:, :], rcnt,
                                           start=True, stop=True).then_inc(hh, 1))
            bis["last_ten"] = kb.val("ten")

    sched = []
    slot_state = {"next": 0}
    done_loads = set()
    done_val = {}
    load_cast_val = {}
    load_left = {}
    stg_state = {"j": 0}
    stg_cast_val = {}
    pieceq = []
    WSEMS = ("w0", "w1s")

    def sched_load(src_ap, cols):
        lid = len(sched)
        kk = src_ap.shape[0] // P
        base = (WS0, WS1)[lid % 2]
        npc = 0
        if cols >= 512:
            pw = 512
            for k in range(kk):
                for c0 in range(0, cols, pw):
                    pieceq.append((lid,
                                   src_ap[k * P:(k + 1) * P, c0:c0 + pw],
                                   base + k * cols + c0, pw, npc))
                    npc += 1
        else:
            kb_ = 512 // cols
            for k0 in range(0, kk, kb_):
                nkb = min(kb_, kk - k0)
                src3 = src_ap[k0 * P:(k0 + nkb) * P, :].rearrange(
                    "(k p) n -> p k n", p=P)
                pieceq.append((lid, src3, base + k0 * cols,
                               nkb * cols, npc))
                npc += 1
        sched.append(cols)
        load_left[lid] = npc

    def drip_piece(n=1):
        for _ in range(n):
            if not pieceq:
                return
            lid, src, dst_off, pw, pidx = pieceq[0]
            if lid >= 2 and (lid - 2) not in done_loads:
                return
            pieceq.pop(0)
            j = stg_state["j"]
            stg_state["j"] += 1
            sem = WSEMS[j % 2]
            ce = ("s", "v")[j % 2]
            ctr = ("sca", "vec")[j % 2]
            if j >= 2:
                kb.wait("s", ("sca", "vec")[j % 2], stg_cast_val[j - 2])
            h, _ = kb.inc(sem, 16)
            dstg = v32(STG + (j % 2) * 512, pw)
            if len(src.shape) == 3:
                dstg = dstg.rearrange("p (k n) -> p k n", k=src.shape[1])
            kb.s(lambda e, d=dstg, s2=src, hh=h:
                 e.dma_start(out=d, in_=s2).then_inc(hh, 16))
            if pidx < 2 and lid >= 2:
                kb.wait(ce, "ten", done_val[lid - 2])
            kb.wait(ce, sem, kb.val(sem))
            h2, _ = kb.inc(ctr)
            cast = (lambda e, d=v16(dst_off, pw),
                    s2=v32(STG + (j % 2) * 512, pw), hh=h2:
                    e.activation(out=d, in_=s2,
                                 func=AF.Copy).then_inc(hh, 1)) \
                if ce == "s" else \
                (lambda e, d=v16(dst_off, pw),
                 s2=v32(STG + (j % 2) * 512, pw), hh=h2:
                 e.tensor_copy(out=d, in_=s2).then_inc(hh, 1))
            (kb.s if ce == "s" else kb.v)(cast)
            stg_cast_val[j] = kb.val(ctr)
            load_cast_val.setdefault(lid, {})[ctr] = kb.val(ctr)
            load_left[lid] -= 1

    def next_slot(cols):
        lid = slot_state["next"]
        slot_state["next"] += 1
        assert sched[lid] == cols
        while load_left[lid]:
            drip_piece()
        drip_bis(1)
        base = (WS0, WS1)[lid % 2]

        def view(k, c0, c1, base=base, cols=cols):
            return v16(base + k * cols + c0, c1 - c0)
        return lid, view, list(load_cast_val[lid].items())

    def slot_done(lid):
        done_loads.add(lid)
        done_val[lid] = kb.val("ten")

    for m_ in range(HID // MG):
        sched_load(dwin[:, m_ * MG:(m_ + 1) * MG], MG)
    for p2_ in range(HEADS):
        for ty_ in (0, 3, 1, 2):
            c0_ = p2_ * 4 * HD + ty_ * HD
            sched_load(dwg[:, c0_:c0_ + HD], HD)
    for m_ in range(DM // MG):
        sched_load(dwout[:, m_ * MG:(m_ + 1) * MG], MG)
    for th_ in range(2):
        for qq_ in range(NQ):
            for mg_ in range(DFQ // MG):
                gc_ = qq_ * DFQ + mg_ * MG
                sched_load(dw1[:, gc_:gc_ + MG], MG)
            for c_ in range(DMC):
                sched_load(dw2[qq_ * DFQ:(qq_ + 1) * DFQ,
                               c_ * P:(c_ + 1) * P], P)

    rot = {"i": 0, "hist": []}

    def mm_group(nchunks, nk, lhsT_fn, rhs_fn, evac_fn, evac_eng, pre_wait):
        gi = rot["i"]
        rot["i"] += 1
        buf = (ps_a, ps_b, ps_c2)[gi % 3]
        nloc = rhs_fn(0).shape[-1]
        if gi >= 3:
            nm, vvv = rot["hist"][gi - 3]
            kb.wait("t", nm, vvv)
        if pre_wait is not None:
            pws = pre_wait if isinstance(pre_wait, list) else [pre_wait]
            for nm, vvv in pws:
                kb.wait("t", nm, vvv)
        for k in range(nk):
            for c in range(nchunks):
                pslice = buf[:, c * 512:c * 512 + nloc]
                st, sp = (k == 0), (k == nk - 1)
                if sp and c == nchunks - 1:
                    h, _ = kb.inc("ten")
                    kb.t(lambda e, p2=pslice, l=lhsT_fn(k, c), r=rhs_fn(k),
                         s1=st, s2=sp, hh=h:
                         e.matmul(p2, l, r, start=s1, stop=s2).then_inc(hh, 1))
                else:
                    kb.t(lambda e, p2=pslice, l=lhsT_fn(k, c), r=rhs_fn(k),
                         s1=st, s2=sp: e.matmul(p2, l, r, start=s1, stop=s2))
        mm_val = kb.val("ten")
        kb.wait(evac_eng, "ten", mm_val)
        for c in range(nchunks):
            evac_fn(c, buf[:, c * 512:c * 512 + nloc])
        ctr = {"v": "vec", "s": "sca"}[evac_eng]
        rot["hist"].append((ctr, kb.val(ctr)))
        drip_piece(4)

    def y_dma(dst, src, sem="ld"):
        h, val = kb.inc(sem, 16)
        kb.y(lambda e, d=dst, s=src, hh=h:
             e.dma_start(out=d, in_=s).then_inc(hh, 16))
        return val

    y_dma(v32(XT, NTC * DM).rearrange("p (j d) -> p j d", j=NTC),
          dx.rearrange("(j p) d -> p j d", p=P))
    y_dma(v32(WRB, DM), dwr)
    y_dma(cbr[:, :], dbr)
    y_dma(cbg[:, :], dbgt)
    y_dma(cb1[:, :], db1)
    y_dma(cb2[:, :], db2)
    y_dma(crms[:, :], drms)
    y_dma(cmq[:, :], dmq)
    y_dma(cidf[:, :], didf)
    y_dma(cones[:, :], dones)
    y_dma(ceps[:, :], deps)
    LD_ALL = kb.val("ld")

    st_sum = v32(XST + 0 * NTC, NTC)
    st_m = v32(XST + 1 * NTC, NTC)
    st_var = v32(XST + 2 * NTC, NTC)
    st_sd = v32(XST + 3 * NTC, NTC)
    st_rs = v32(XST + 4 * NTC, NTC)
    st_nm = v32(XST + 5 * NTC, NTC)
    st_sq = v32(XST + 6 * NTC, NTC)
    st_t = v32(XST + 7 * NTC, NTC)

    kb.wait("v", "ld", LD_ALL)
    kb.wait("s", "ld", LD_ALL)
    for j in range(NTC):
        kb.v(lambda e, j=j: e.reduce_sum(
            out=v32(XST, NTC)[:, j:j + 1], in_=v32(XT + j * DM, DM),
            axis=mybir.AxisListType.X))
        h, _ = kb.inc("sca")
        kb.s(lambda e, j=j, hh=h: e.activation(
            out=v32(XSCR, DM), in_=v32(XT + j * DM, DM), func=AF.Square,
            accum_out=v32(XST + 6 * NTC, NTC)[:, j:j + 1]).then_inc(hh, 1))
        sd()
    SQ_ALL = kb.val("sca")
    kb.wait("v", "sca", SQ_ALL)
    vd()
    kb.v(lambda e: e.tensor_scalar(out=st_m, in0=st_sum, scalar1=1.0 / DM,
                                   scalar2=None, op0=ALU.mult))
    vd()
    kb.v(lambda e: e.tensor_scalar(out=st_var, in0=st_sq, scalar1=1.0 / DM,
                                   scalar2=None, op0=ALU.mult))
    vd()
    kb.v(lambda e: e.scalar_tensor_tensor(out=st_t, in0=st_m, scalar=-1.0,
                                          in1=st_m, op0=ALU.mult, op1=ALU.mult))
    vd()
    h, _ = kb.inc("vec")
    kb.v(lambda e, hh=h: e.tensor_tensor(out=st_var, in0=st_var, in1=st_t,
                                         op=ALU.add).then_inc(hh, 1))
    kb.wait("s", "vec", kb.val("vec"))
    h, _ = kb.inc("sca")
    kb.s(lambda e, hh=h: e.activation(out=st_sd, in_=st_var, func=AF.Sqrt,
                                      bias=ceps[:, 0:1]).then_inc(hh, 1))
    kb.wait("v", "sca", kb.val("sca"))
    kb.v(lambda e: e.reciprocal(out=st_rs, in_=st_sd))
    vd()
    kb.v(lambda e: e.scalar_tensor_tensor(out=st_nm, in0=st_m, scalar=-1.0,
                                          in1=st_rs, op0=ALU.mult, op1=ALU.mult))
    vd()
    for j in range(NTC):
        kb.v(lambda e, j=j: e.tensor_scalar(
            out=v32(XT + j * DM, DM), in0=v32(XT + j * DM, DM),
            scalar1=v32(XST + 4 * NTC, NTC)[:, j:j + 1],
            scalar2=v32(XST + 5 * NTC, NTC)[:, j:j + 1],
            op0=ALU.mult, op1=ALU.add))
    vd()
    h, _ = kb.inc("vec")
    kb.v(lambda e, hh=h: e.engine_nop().then_inc(hh, 1))
    V_XN = kb.val("vec")
    for j in range(NTC):
        kb.v(lambda e, j=j: e.tensor_tensor(out=v32(XSCR, DM),
                                            in0=v32(XT + j * DM, DM),
                                            in1=v32(WRB, DM), op=ALU.mult))
        vd()
        kb.v(lambda e, j=j: e.reduce_sum(out=v32(SC_, NTC)[:, j:j + 1],
                                         in_=v32(XSCR, DM),
                                         axis=mybir.AxisListType.X))
        vd()
    h, _ = kb.inc("vec")
    kb.v(lambda e, hh=h: e.tensor_scalar(out=v32(SC_, NTC), in0=v32(SC_, NTC),
                                         scalar1=cbr[:, 0:1], scalar2=None,
                                         op0=ALU.add).then_inc(hh, 1))
    V_SC = kb.val("vec")
    kb.v(lambda e: e.tensor_scalar(out=cbgh[:, :], in0=cbg[:, :], scalar1=0.5,
                                   scalar2=None, op0=ALU.mult))
    kb.v(lambda e: e.memset(v32(RZ, TOKC), 0.0))
    h, _ = kb.inc("vec")
    kb.v(lambda e, hh=h: e.engine_nop().then_inc(hh, 1))
    V_PREP = kb.val("vec")

    kb.wait("y", "vec", V_SC)
    kb.wait("y", "ccm", kb.val("ccm"))
    h, _ = kb.inc("ccm", 16)
    kb.y(lambda e, hh=h: e.dma_start(out=sc_in.ap(),
                                     in_=v32(SC_, NTC)).then_inc(hh, 16))
    kb.wait("g", "ccm", kb.val("ccm"))
    h, _ = kb.inc("cc")
    kb.g(lambda e, hh=h: e.collective_compute(
        "AllGather", ALU.bypass, replica_groups=[[0, 1, 2, 3], [4, 5, 6, 7]],
        ins=[sc_in.ap().opt()], outs=[sc_out.ap().opt()]).then_inc(hh, 1))
    kb.wait("y", "cc", kb.val("cc"))
    kb.wait("y", "ccm", kb.val("ccm"))
    h, _ = kb.inc("ccm", 16)
    kb.y(lambda e, hh=h: e.dma_start(
        out=v32(AGS, 4 * NTC).rearrange("p (c j) -> p c j", c=4),
        in_=sc_out.ap().rearrange("(c p) j -> p c j", p=P)).then_inc(hh, 16))
    CCM_SC2 = kb.val("ccm")

    kb.wait("t", "vec", V_XN)
    TP_SLOTS = [ps_a[:, 0:P], ps_a[:, 512:512 + P],
                ps_b[:, 0:P], ps_b[:, 512:512 + P]]
    S_TPBASE = kb.val("sca")
    n_tp = 0
    for j in range(NTC):
        for d in range(DMC):
            tpi = n_tp
            n_tp += 1
            slot = TP_SLOTS[tpi % 4]
            if tpi >= 4:
                kb.wait("t", "sca", S_TPBASE + tpi - 3)
            h, _ = kb.inc("ten")
            kb.t(lambda e, sl=slot, j=j, d=d, hh=h: e.transpose(
                sl, v32(XT + j * DM + d * P, P), cidf[:, :]).then_inc(hh, 1))
            kb.wait("s", "ten", kb.val("ten"))
            h, _ = kb.inc("sca")
            kb.s(lambda e, sl=slot, j=j, d=d, hh=h: e.activation(
                out=v16(XN + d * TOKC + j * P, P), in_=sl,
                func=AF.Copy).then_inc(hh, 1))
    S_XNFM = kb.val("sca")
    kb.wait("s", "sca", S_XNFM)
    h, _ = kb.inc("xnw", 16)
    kb.s(lambda e, hh=h: e.dma_start(out=xn_hbm.ap(),
                                     in_=v16(XN, NTC * DM)).then_inc(hh, 16))

    kb.wait("t", "sca", S_XNFM)
    for m in range(HID // MG):
        si, wv, wguard = next_slot(MG)
        nch = MG // P
        for cp in range(0, nch, 2):
            ncp = min(2, nch - cp)
            for n in range(2):
                def evac(c, pslice, m=m, cp=cp, n=n, nch=nch):
                    ch = m * nch + cp + c
                    h, _ = kb.inc("sca")
                    kb.s(lambda e, p2=pslice, ch=ch, n=n, hh=h: e.activation(
                        out=v16(XP + ch * TOKC + n * NMM, NMM), in_=p2,
                        func=AF.Copy).then_inc(hh, 1))
                mm_group(ncp, DMC,
                         lambda k, c, wv=wv, cp=cp: wv(k, (cp + c) * P,
                                                       (cp + c + 1) * P),
                         lambda k, n=n: v16(XN + k * TOKC + n * NMM, NMM),
                         evac, "s", pre_wait=list(wguard))
        slot_done(si)
        hard_sync(2)
    S_XP = kb.val("sca")

    def fsl(ct, c0, n):
        return v32(RF + ct * TOKC + c0, n)

    def gsl(ct, c0, n):
        return v32(RG + ct * TOKC + c0, n)

    def iosl(ct, c0, n):
        return v32(RIO + ct * TOKC + c0, n)

    def csl(ct, c0, n):
        return v32(RC + ct * TOKC + c0, n)

    V_LASTCSCAN = 0
    last_hw = 0
    for p_ in range(HEADS):
        for ty in (0, 3, 1, 2):
            col0 = p_ * 4 * HD + ty * HD
            si, wv, wguard = next_slot(HD)
            if ty == 2:
                VG = kb.val("vec")
            for cp in range(0, HDC, 2):
                ncp = min(2, HDC - cp)
                for n in range(2):
                    first = (cp == 0 and n == 0)

                    def evac(c, pslice, ty=ty, cp=cp, n=n, p_=p_, first=first):
                        ct = cp + c
                        gidx = (p_ * 4 * HD + ty * HD) // P + ct
                        if ty == 0 and first and c == 0:
                            kb.wait("s", "vec", V_LASTCSCAN)
                        if ty == 2 and first and c == 0:
                            kb.wait("s", "vec", VG)
                        if ty == 0:
                            h, _ = kb.inc("sca")
                            kb.s(lambda e, p2=pslice, d=iosl(ct, n * NMM, NMM),
                                 g2=gidx, hh=h: e.activation(
                                     out=d, in_=p2, func=AF.Exp,
                                     bias=cbg[:, g2:g2 + 1]).then_inc(hh, 1))
                        elif ty == 3:
                            h, _ = kb.inc("sca")
                            kb.s(lambda e, p2=pslice, d=gsl(ct, n * NMM, NMM),
                                 g2=gidx, hh=h: e.activation(
                                     out=d, in_=p2, func=AF.Tanh,
                                     bias=cbg[:, g2:g2 + 1]).then_inc(hh, 1))
                            kb.wait("v", "sca", kb.val("sca"))
                            h2, _ = kb.inc("vec")
                            kb.v(lambda e, ct=ct, n=n, hh=h2: e.tensor_tensor(
                                out=gsl(ct, n * NMM, NMM),
                                in0=iosl(ct, n * NMM, NMM),
                                in1=gsl(ct, n * NMM, NMM),
                                op=ALU.mult).then_inc(hh, 1))
                        else:
                            dst = (fsl if ty == 1 else iosl)(ct, n * NMM, NMM)
                            h, _ = kb.inc("sca")
                            kb.s(lambda e, p2=pslice, d=dst, g2=gidx, hh=h:
                                 e.activation(out=d, in_=p2, func=AF.Tanh,
                                              scale=0.5,
                                              bias=cbgh[:, g2:g2 + 1]).then_inc(hh, 1))
                            kb.wait("v", "sca", kb.val("sca"))
                            h2, _ = kb.inc("vec")
                            kb.v(lambda e, d=dst, hh=h2: e.tensor_scalar(
                                out=d, in0=d, scalar1=0.5, scalar2=0.5,
                                op0=ALU.mult, op1=ALU.add).then_inc(hh, 1))
                    pw = list(wguard)
                    if p_ == 0 and ty == 0 and cp == 0 and n == 0:
                        pw.append(("sca", S_XP))
                    mm_group(ncp, HIDC,
                             lambda k, c, wv=wv, cp=cp: wv(k, (cp + c) * P,
                                                           (cp + c + 1) * P),
                             lambda k, n=n: v16(XP + k * TOKC + n * NMM, NMM),
                             evac, "s", pre_wait=pw)
            slot_done(si)
            hard_sync(2)
        S_GATES = kb.val("sca")
        V_GATES = kb.val("vec")

        kb.wait("v", "sca", S_GATES)
        for ct in range(HDC):
            kb.v(lambda e, ct=ct: e.tensor_tensor_scan(
                out=v32(RSA, TOKC), data0=fsl(ct, 0, TOKC),
                data1=gsl(ct, 0, TOKC), initial=0.0,
                op0=ALU.mult, op1=ALU.add))
            vd()
            kb.v(lambda e, ct=ct: e.tensor_copy(
                out=v32(SUMM + ct, 1), in_=v32(RSA + TOKC - 1, 1)))
            kb.v(lambda e, ct=ct: e.tensor_tensor_scan(
                out=v32(RSB, TOKC), data0=fsl(ct, 0, TOKC),
                data1=v32(RZ, TOKC), initial=1.0,
                op0=ALU.mult, op1=ALU.add))
            vd()
            kb.v(lambda e, ct=ct: e.tensor_copy(
                out=v32(SUMM + HDC + ct, 1), in_=v32(RSB + TOKC - 1, 1)))
        h, _ = kb.inc("vec")
        kb.v(lambda e, hh=h: e.engine_nop().then_inc(hh, 1))
        V_SUMM = kb.val("vec")
        hard_sync(2)

        kb.wait("y", "vec", V_SUMM)
        kb.wait("y", "ccm", kb.val("ccm"))
        h, _ = kb.inc("ccm", 16)
        kb.y(lambda e, p_=p_, hh=h: e.dma_start(
            out=ag_in[p_].ap(), in_=v32(SUMM, 2 * HDC)).then_inc(hh, 16))
        kb.wait("g", "ccm", kb.val("ccm"))
        h, _ = kb.inc("cc")
        kb.g(lambda e, p_=p_, hh=h: e.collective_compute(
            "AllGather", ALU.bypass,
            replica_groups=[[0, 1, 2, 3], [4, 5, 6, 7]],
            ins=[ag_in[p_].ap().opt()],
            outs=[ag_out[p_].ap().opt()]).then_inc(hh, 1))
        kb.wait("y", "cc", kb.val("cc"))
        kb.wait("y", "ccm", kb.val("ccm"))
        h, _ = kb.inc("ccm", 16)
        kb.y(lambda e, p_=p_, hh=h: e.dma_start(
            out=v32(AGT, 4 * 2 * HDC).rearrange("p (c j) -> p c j", c=4),
            in_=ag_out[p_].ap().rearrange("(c p) j -> p c j", p=P)).then_inc(hh, 16))
        CCM_SUM = kb.val("ccm")

        if cfg.get("debug"):
            if p_ == 0:
                dbg_ag = nc.dram_tensor("dbg_ag", [P, HEADS * 8 * HDC], F32,
                                        kind="ExternalOutput")
                build.dbg_ag = dbg_ag
            kb.wait("y", "ccm", CCM_SUM)
            kb.wait("y", "ccm", kb.val("ccm"))
            h, _ = kb.inc("ccm", 16)
            kb.y(lambda e, p_=p_, hh=h: e.dma_start(
                out=build.dbg_ag.ap()[:, p_ * 8 * HDC:(p_ + 1) * 8 * HDC],
                in_=v32(AGT, 4 * 2 * HDC)).then_inc(hh, 16))
            CCM_SUM = kb.val("ccm")
        kb.wait("v", "ccm", CCM_SUM)
        W = 2 * HDC
        t1, t2, t3 = v32(T1_, 1), v32(T2_, 1), v32(T3_, 1)
        for ct in range(HDC):
            s0 = v32(AGT + 0 * W + ct, 1)
            s1_ = v32(AGT + 1 * W + ct, 1)
            s2_ = v32(AGT + 2 * W + ct, 1)
            p1 = v32(AGT + 1 * W + HDC + ct, 1)
            p2_ = v32(AGT + 2 * W + HDC + ct, 1)
            cin = v32(CIN + ct, 1)
            kb.v(lambda e, a=t1, b=s0: e.tensor_copy(out=a, in_=b))
            vd()
            kb.v(lambda e, a=t2, b=p1, c=t1, d=s1_: e.scalar_tensor_tensor(
                out=a, in0=b, scalar=c[:, 0:1], in1=d,
                op0=ALU.mult, op1=ALU.add))
            vd()
            kb.v(lambda e, a=t3, b=p2_, c=t2, d=s2_: e.scalar_tensor_tensor(
                out=a, in0=b, scalar=c[:, 0:1], in1=d,
                op0=ALU.mult, op1=ALU.add))
            vd()
            kb.v(lambda e, a=cin, b=t1: e.tensor_tensor(
                out=a, in0=b, in1=cmq[:, 0:1], op=ALU.mult))
            vd()
            kb.v(lambda e, a=cin, b=t2: e.scalar_tensor_tensor(
                out=a, in0=b, scalar=cmq[:, 1:2], in1=a,
                op0=ALU.mult, op1=ALU.add))
            vd()
            kb.v(lambda e, a=cin, b=t3: e.scalar_tensor_tensor(
                out=a, in0=b, scalar=cmq[:, 2:3], in1=a,
                op0=ALU.mult, op1=ALU.add))
            vd()
            kb.v(lambda e, ct=ct, a=cin: e.tensor_tensor_scan(
                out=csl(ct, 0, TOKC), data0=fsl(ct, 0, TOKC),
                data1=gsl(ct, 0, TOKC), initial=a[:, 0:1],
                op0=ALU.mult, op1=ALU.add))
            vd()
        h, _ = kb.inc("vec")
        kb.v(lambda e, hh=h: e.engine_nop().then_inc(hh, 1))
        V_LASTCSCAN = kb.val("vec")
        hard_sync(2)

        kb.wait("s", "vec", V_LASTCSCAN)
        kb.wait("v", "hw", last_hw)
        for n in range(2):
            if n:
                kb.wait("s", "vec", kb.val("vec"))
            for ct in range(HDC):
                h, _ = kb.inc("sca")
                kb.s(lambda e, ct=ct, n=n, hh=h: e.activation(
                    out=v32(RCSQ + (ct % 2) * NMM, NMM),
                    in_=csl(ct, n * NMM, NMM),
                    func=AF.Square).then_inc(hh, 1))
                kb.wait("t", "sca", kb.val("sca"))
                h2, _ = kb.inc("ten")
                kb.t(lambda e, ct=ct, hh=h2: e.matmul(
                    ps_c1[:, 0:NMM], cones[:, :],
                    v32(RCSQ + (ct % 2) * NMM, NMM),
                    start=(ct == 0), stop=(ct == HDC - 1)).then_inc(hh, 1))
                if ct + 2 < HDC:
                    kb.wait("s", "ten", kb.val("ten"))
            kb.wait("s", "ten", kb.val("ten"))
            h, _ = kb.inc("sca")
            kb.s(lambda e, hh=h: e.activation(
                out=v32(RSD, NMM), in_=ps_c1[:, 0:NMM], func=AF.Sqrt,
                scale=1.0 / HD, bias=ceps[:, 1:2]).then_inc(hh, 1))
            kb.wait("v", "sca", kb.val("sca"))
            h, _ = kb.inc("vec")
            kb.v(lambda e, n=n, hh=h: e.reciprocal(
                out=v32(RRS + n * NMM, NMM),
                in_=v32(RSD, NMM)).then_inc(hh, 1))
            vd()
            kb.wait("t", "vec", kb.val("vec"))
            for ct in range(HDC):
                h, _ = kb.inc("vec")
                kb.v(lambda e, ct=ct, n=n, hh=h: e.scalar_tensor_tensor(
                    out=v32(RU, NMM), in0=csl(ct, n * NMM, NMM),
                    scalar=crms[:, ct:ct + 1], in1=v32(RRS + n * NMM, NMM),
                    op0=ALU.mult, op1=ALU.mult).then_inc(hh, 1))
                kb.wait("s", "vec", kb.val("vec"))
                h, _ = kb.inc("sca")
                kb.s(lambda e, hh=h: e.activation(
                    out=v32(RTH, NMM), in_=v32(RU, NMM),
                    func=AF.Tanh).then_inc(hh, 1))
                kb.wait("v", "sca", kb.val("sca"))
                h, _ = kb.inc("vec")
                kb.v(lambda e, ct=ct, n=n, hh=h: e.tensor_tensor(
                    out=v16(HS + ct * TOKC + n * NMM, NMM),
                    in0=iosl(ct, n * NMM, NMM),
                    in1=v32(RTH, NMM), op=ALU.mult).then_inc(hh, 1))
        h, _ = kb.inc("vec")
        kb.v(lambda e, hh=h: e.engine_nop().then_inc(hh, 1))
        V_H = kb.val("vec")
        V_LASTCSCAN = V_H
        hard_sync(1)
        kb.wait("s", "vec", V_H)
        h, _ = kb.inc("hw", 16)
        kb.s(lambda e, p_=p_, hh=h: e.dma_start(
            out=h_hbm.ap()[:, p_ * HDC * TOKC:(p_ + 1) * HDC * TOKC],
            in_=v16(HS, HDC * TOKC)).then_inc(hh, 16))
        last_hw = kb.val("hw")

    if cfg.get("debug"):
        dbg_r = nc.dram_tensor("dbg_r", [P, 4 * HDC * TOKC], F32,
                               kind="ExternalOutput")
        dbg_xpe = nc.dram_tensor("dbg_xpe", [P, HIDC * TOKC], BF16,
                                 kind="ExternalOutput")
        kb.wait("y", "vec", kb.val("vec"))
        h, _ = kb.inc("outw", 16)
        kb.y(lambda e, hh=h: e.dma_start(out=dbg_r.ap(),
                                         in_=v32(RF, 4 * HDC * TOKC)).then_inc(hh, 16))
        h, _ = kb.inc("outw", 16)
        kb.y(lambda e, hh=h: e.dma_start(out=dbg_xpe.ap(),
                                         in_=v16(XP, HIDC * TOKC)).then_inc(hh, 16))
        DBGR_V = kb.val("outw")
        kb.wait("v", "outw", DBGR_V)
        kb.wait("s", "outw", DBGR_V)
    STOP = cfg.get("stop_after")
    if STOP == "R":
        kb.wait("y", "hw", kb.val("hw"))
        kb.wait("y", "vec", kb.val("vec"))
        return kb.emit()
    drip_bis(NITER)
    kb.wait("v", "ten", bis["last_ten"])
    kb.v(lambda e: e.tensor_scalar(out=v32(BD1, 1), in0=ps_m[:, 0:1],
                                   scalar1=float(KSEL), scalar2=None,
                                   op0=ALU.is_ge))
    kb.v(lambda e: e.tensor_tensor(out=v32(BD2, 1), in0=mid[:, :],
                                   in1=lo[:, :], op=ALU.subtract))
    vd()
    kb.v(lambda e: e.scalar_tensor_tensor(
        out=lo[:, :], in0=v32(BD2, 1), scalar=v32(BD1, 1)[:, 0:1],
        in1=lo[:, :], op0=ALU.mult, op1=ALU.add))
    vd()
    h, _ = kb.inc("vec")
    kb.v(lambda e, hh=h: e.tensor_scalar(
        out=v32(MSK, NTC), in0=v32(SC_, NTC), scalar1=lo[:, 0:1],
        scalar2=None, op0=ALU.is_ge).then_inc(hh, 1))
    GP_MASK = kb.val("vec")
    STRUNC = cfg.get("s_trunc")

    def dump_arenas():
        if not cfg.get("dump_arenas"):
            return
        d16 = nc.dram_tensor("dbg_a16d", [P, N16], BF16,
                             kind="ExternalOutput")
        d32 = nc.dram_tensor("dbg_a32d", [P, N32], F32,
                             kind="ExternalOutput")
        kb.wait("y", "vec", kb.val("vec"))
        kb.wait("y", "sca", kb.val("sca"))
        h2, _ = kb.inc("outw", 16)
        kb.y(lambda e, hh=h2: e.dma_start(out=d16.ap(),
                                          in_=a16[:, :]).then_inc(hh, 16))
        h2, _ = kb.inc("outw", 16)
        kb.y(lambda e, hh=h2: e.dma_start(out=d32.ap(),
                                          in_=a32[:, :]).then_inc(hh, 16))
        kb.wait("y", "outw", kb.val("outw"))

    if STRUNC == 1:
        kb.wait("y", "hw", kb.val("hw"))
        kb.wait("y", "vec", kb.val("vec"))
        dump_arenas()
        return kb.emit()

    kb.wait("y", "hw", last_hw)
    h, _ = kb.inc("hr", 16)
    kb.y(lambda e, hh=h: e.dma_start(out=v16(XP, HIDC * TOKC),
                                     in_=h_hbm.ap()).then_inc(hh, 16))
    if STRUNC == 2:
        kb.wait("y", "hr", kb.val("hr"))
        dump_arenas()
        return kb.emit()
    kb.wait("t", "hr", kb.val("hr"))
    kb.wait("s", "hw", last_hw)
    nch_s = MG // P
    for m in range(DM // MG):
        si, wv, wguard = next_slot(MG)
        for cp in range(0, nch_s, 2):
            ncp = min(2, nch_s - cp)
            for n in range(2):
                def evac(c, pslice, cp=cp, n=n):
                    lc = cp + c
                    h, _ = kb.inc("sca")
                    kb.s(lambda e, p2=pslice, lc=lc, n=n, hh=h: e.activation(
                        out=v16(HS + lc * TOKC + n * NMM, NMM), in_=p2,
                        func=AF.Copy).then_inc(hh, 1))
                mm_group(ncp, HIDC,
                         lambda k, c, wv=wv, cp=cp: wv(k, (cp + c) * P,
                                                       (cp + c + 1) * P),
                         lambda k, n=n: v16(XP + k * TOKC + n * NMM, NMM),
                         evac, "s", pre_wait=list(wguard))
        slot_done(si)
        kb.wait("s", "sca", kb.val("sca"))
        h, _ = kb.inc("sw", 16)
        kb.s(lambda e, m=m, hh=h: e.dma_start(
            out=s_hbm.ap()[:, m * nch_s * TOKC:(m + 1) * nch_s * TOKC],
            in_=v16(HS, nch_s * TOKC)).then_inc(hh, 16))
        kb.wait("s", "sw", kb.val("sw"))

    if STOP == "S":
        kb.wait("y", "sw", kb.val("sw"))
        return kb.emit()
    kb.wait("y", "xnw", 16)
    h, _ = kb.inc("xnr", 16)
    kb.y(lambda e, hh=h: e.dma_start(out=v16(XN, NTC * DM),
                                     in_=xn_hbm.ap()).then_inc(hh, 16))
    kb.wait("t", "xnr", kb.val("xnr"))
    kb.wait("v", "vec", GP_MASK) if False else None
    kb.wait("v", "sw", kb.val("sw"))

    ssl_loaded = {}
    prd_done = {}
    out_done = {}
    fo_ready = {}
    last_silu = {"v": 0}
    out_dma_val = {}
    V_REND = kb.val("vec")
    first_xr = [True]
    xr_cnt = {"n": 0}
    FTP_SLOTS = [ps_c1[:, 0:P], ps_m[:, 0:P]]
    n_ftp = {"n": 0}
    for th in range(2):
        for qq in range(NQ):
            for mg in range(DFQ // MG):
                gcol = qq * DFQ + mg * MG
                si, wv, wguard = next_slot(MG)
                nch = MG // P
                for cp in range(0, nch, 2):
                    ncp = min(2, nch - cp)

                    def evac(c, pslice, mg=mg, cp=cp, qq=qq, nch=nch):
                        lc = mg * nch + cp + c
                        gch = qq * DFQC + lc
                        h, _ = kb.inc("sca")
                        kb.s(lambda e, p2=pslice, lc=lc, gch=gch, hh=h:
                             e.activation(out=v16(H1 + lc * NMM, NMM), in_=p2,
                                          func=AF.Gelu_apprx_tanh,
                                          bias=cb1[:, gch:gch + 1]).then_inc(hh, 1))
                    mm_group(ncp, DMC,
                             lambda k, c, wv=wv, cp=cp: wv(k, (cp + c) * P,
                                                           (cp + c + 1) * P),
                             lambda k, th=th: v16(XN + k * TOKC + th * TOKH,
                                                  TOKH),
                             evac, "s", pre_wait=list(wguard))
                slot_done(si)
            S_H1 = kb.val("sca")
            first_w2 = [True]
            for c in range(DMC):
                si, wv, wguard = next_slot(P)
                if qq == NQ - 1:
                    key = (th, c)
                    kb.wait("y", "sw", kb.val("sw"))
                    kb.wait("y", "ssl", kb.val("ssl"))
                    pk = (th, c - 2) if c >= 2 else (1 - th, DMC - 2 + c)
                    if pk in prd_done:
                        kb.wait("y", "vec", prd_done[pk])
                    h, _ = kb.inc("ssl", 16)
                    kb.y(lambda e, c=c, th=th, hh=h: e.dma_start(
                        out=v16(SSL + (c % 2) * TOKH, TOKH),
                        in_=s_hbm.ap()[:, c * TOKC + th * TOKH:
                                       c * TOKC + th * TOKH + TOKH]).then_inc(hh, 16))
                    ssl_loaded[key] = kb.val("ssl")

                def evac2(cc, pslice, c=c, qq=qq, th=th):
                    if qq == 0:
                        h, _ = kb.inc("vec")
                        kb.v(lambda e, p2=pslice, c=c, hh=h: e.tensor_scalar(
                            out=v32(FO + c * NMM, NMM), in0=p2,
                            scalar1=cb2[:, c:c + 1], scalar2=None,
                            op0=ALU.add).then_inc(hh, 1))
                        fo_ready[(th, c)] = kb.val("vec")
                    elif qq < NQ - 1:
                        if qq == 1:
                            kb.wait("v", "vec", fo_ready[(th, c)])
                        h, _ = kb.inc("vec")
                        kb.v(lambda e, p2=pslice, c=c, hh=h: e.tensor_tensor(
                            out=v32(FO + c * NMM, NMM), in0=p2,
                            in1=v32(FO + c * NMM, NMM),
                            op=ALU.add).then_inc(hh, 1))
                    else:
                        kb.wait("v", "sca", last_silu["v"])
                        h, _ = kb.inc("vec")
                        kb.v(lambda e, p2=pslice, c=c, hh=h: e.tensor_tensor(
                            out=v32(FU, NMM), in0=p2,
                            in1=v32(FO + c * NMM, NMM),
                            op=ALU.add).then_inc(hh, 1))
                        kb.wait("s", "vec", kb.val("vec"))
                        h, _ = kb.inc("sca")
                        kb.s(lambda e, hh=h: e.activation(
                            out=v32(FGS, NMM), in_=v32(FU, NMM),
                            func=AF.Silu).then_inc(hh, 1))
                        last_silu["v"] = kb.val("sca")
                        kb.wait("v", "sca", kb.val("sca"))
                        kb.wait("v", "ssl", ssl_loaded[(th, c)])
                        h, _ = kb.inc("vec")
                        kb.v(lambda e, c=c, hh=h: e.tensor_tensor(
                            out=v16(HS + c * NMM, NMM), in0=v32(FGS, NMM),
                            in1=v16(SSL + (c % 2) * TOKH, TOKH),
                            op=ALU.mult).then_inc(hh, 1))
                        prd_done[(th, c)] = kb.val("vec")
                ee = "v"
                pw2 = list(wguard)
                if first_w2[0]:
                    pw2.append(("sca", S_H1))
                    first_w2[0] = False
                mm_group(1, DFQC,
                         lambda k, cc, wv=wv: wv(k, 0, P),
                         lambda k: v16(H1 + k * NMM, NMM),
                         evac2, ee, pre_wait=pw2)
                slot_done(si)
        V_PRODALL = kb.val("vec")
        for j in range(NTH):
            J = th * NTH + j
            kb.wait("y", "xr", kb.val("xr"))
            if first_xr[0]:
                kb.wait("y", "vec", V_REND)
                first_xr[0] = False
            if J >= 2 and (J - 2) in out_done:
                kb.wait("y", "vec", out_done[J - 2])
            h, _ = kb.inc("xr", 16)
            kb.y(lambda e, J=J, hh=h: e.dma_start(
                out=v32(FXJ + (J % 2) * DM, DM),
                in_=dx[J * P:(J + 1) * P, :]).then_inc(hh, 16))
            XRV = kb.val("xr")
            kb.wait("v", "xr", XRV)
            if (J - 2) in out_dma_val:
                kb.wait("v", "outw", out_dma_val[J - 2])
            for c in range(DMC):
                tpi = n_ftp["n"]
                n_ftp["n"] += 1
                slot = FTP_SLOTS[tpi % 2]
                cast = v32(FTP + (tpi % 2) * P, P)
                if tpi >= 2:
                    kb.wait("v", "ten", tp_ten[tpi - 2])
                kb.wait("v", "vec", V_PRODALL)
                h, _ = kb.inc("vec")
                kb.v(lambda e, cst=cast, c=c, j=j, hh=h: e.tensor_copy(
                    out=cst, in_=v16(HS + c * NMM + j * P, P)).then_inc(hh, 1))
                kb.wait("t", "vec", kb.val("vec"))
                if tpi >= 2:
                    kb.wait("t", "vec", tp_evac[tpi - 2])
                h, _ = kb.inc("ten")
                kb.t(lambda e, sl=slot, cst=cast, hh=h: e.transpose(
                    sl, cst, cidf[:, :]).then_inc(hh, 1))
                if tpi == 0:
                    tp_ten, tp_evac = {}, {}
                tp_ten[tpi] = kb.val("ten")
                kb.wait("v", "ten", tp_ten[tpi])
                h, _ = kb.inc("vec")
                kb.v(lambda e, sl=slot, c=c, J=J, hh=h:
                     e.scalar_tensor_tensor(
                         out=v32(FOJ + (J % 2) * DM + c * P, P), in0=sl,
                         scalar=v32(MSK + J, 1)[:, 0:1],
                         in1=v32(FXJ + (J % 2) * DM + c * P, P),
                         op0=ALU.mult, op1=ALU.add).then_inc(hh, 1))
                tp_evac[tpi] = kb.val("vec")
                drip_piece(1)
            out_done[J] = kb.val("vec")
            kb.wait("y", "vec", out_done[J])
            h, _ = kb.inc("outw", 16)
            kb.y(lambda e, J=J, hh=h: e.dma_start(
                out=dout[J * P:(J + 1) * P, :],
                in_=v32(FOJ + (J % 2) * DM, DM)).then_inc(hh, 16))
            out_dma_val[J] = kb.val("outw")

    if cfg.get("debug"):
        dbg_xn = nc.dram_tensor("dbg_xn", [P, NTC * DM], BF16,
                                kind="ExternalOutput")
        dbg_h = nc.dram_tensor("dbg_h", [P, HIDC * TOKC], BF16,
                               kind="ExternalOutput")
        dbg_s = nc.dram_tensor("dbg_s", [P, DMC * TOKC], BF16,
                               kind="ExternalOutput")
        dbg_t = nc.dram_tensor("dbg_tail", [P, 224], F32,
                               kind="ExternalOutput")
        dbg_fo = nc.dram_tensor("dbg_fo", [P, DMC * NMM], F32,
                                kind="ExternalOutput")
        dbg_xp = nc.dram_tensor("dbg_xp", [P, HIDC * TOKC], BF16,
                                kind="ExternalOutput")
        kb.wait("y", "outw", kb.val("outw"))
        for dst, srcap in ((dbg_xn.ap(), xn_hbm.ap()),
                           (dbg_h.ap(), h_hbm.ap()),
                           (dbg_s.ap(), s_hbm.ap()),
                           (dbg_t.ap(), v32(PT, 224)),
                           (dbg_fo.ap(), v32(FO, DMC * NMM)),
                           (dbg_xp.ap(), v16(XP, HIDC * TOKC))):
            h, _ = kb.inc("outw", 16)
            kb.y(lambda e, d=dst, s2=srcap, hh=h:
                 e.dma_start(out=d, in_=s2).then_inc(hh, 16))
    kb.wait("y", "outw", kb.val("outw"))
    return kb.emit()


def _layouts(inputs, cfg):
    B, SEQ, DM, HID, HEADS, DF = (cfg[k] for k in
                                  ("B", "SEQ", "DM", "HID", "HEADS", "DF"))
    TOKC = SEQ // 4
    DMC, HIDC, DFC = DM // P, HID // P, DF // P
    HD = HID // HEADS
    HDC = HD // P
    GC = 4 * HID // P
    f32 = np.float32
    x = np.asarray(inputs["L0_attn_x"], f32)
    shared = dict(
        winT=np.ascontiguousarray(np.asarray(inputs["L0_attn_W_in"], f32).T),
        wgT=np.ascontiguousarray(np.asarray(inputs["L0_attn_W_gate"], f32).T),
        woutT=np.ascontiguousarray(np.asarray(inputs["L0_attn_W_out"], f32).T),
        w1=np.ascontiguousarray(np.asarray(inputs["L0_ffn_w1"], f32)),
        w2=np.ascontiguousarray(np.asarray(inputs["L0_ffn_w2"], f32)),
        bg=np.ascontiguousarray(
            np.asarray(inputs["L0_attn_b_gate"], f32).reshape(GC, P).T),
        b1c=np.ascontiguousarray(
            np.asarray(inputs["L0_ffn_b1"], f32).reshape(DFC, P).T),
        b2c=np.ascontiguousarray(
            np.asarray(inputs["L0_ffn_b2"], f32).reshape(DMC, P).T),
        rmsc=np.ascontiguousarray(
            np.asarray(inputs["L0_attn_rms_w"], f32).reshape(HDC, P).T),
        wrb=np.ascontiguousarray(
            np.broadcast_to(np.asarray(inputs["L0_ffn_w_r"], f32), (P, DM))),
        brb=np.full((P, 1), np.asarray(inputs["L0_ffn_b_r"], f32), f32),
        identf=np.eye(P, dtype=f32),
        onespp=np.ones((P, P), f32),
        epsc=np.broadcast_to(np.asarray(
            [cfg["LN_EPS"], cfg["RMS_EPS"]], f32), (P, 2)).copy(),
    )
    in_maps = []
    for core in range(8):
        b, q = core // 4, core % 4
        mq = np.zeros((P, 3), f32)
        if q >= 1:
            mq[:, q - 1] = 1.0
        m = dict(shared)
        m["x_own"] = np.ascontiguousarray(x[b, q * TOKC:(q + 1) * TOKC, :])
        m["mq"] = mq
        in_maps.append(m)
    return in_maps


def kernel(**inputs):
    cfg = FULL_CFG
    in_maps = _layouts(inputs, cfg)
    nc = build(cfg)
    res = run_bass_kernel_spmd(nc, in_maps, core_ids=list(range(8)))
    B, SEQ, DM = cfg["B"], cfg["SEQ"], cfg["DM"]
    TOKC = SEQ // 4
    out = np.empty((B, SEQ, DM), np.float32)
    for core in range(8):
        b, q = core // 4, core % 4
        out[b, q * TOKC:(q + 1) * TOKC, :] = res.results[core]["out"]
    return out


# version 06
# speedup vs baseline: 1.0357x; 1.0357x over previous
import os
import numpy as np
import ml_dtypes
import concourse.bass as bass
import concourse.mybir as mybir
from concourse.bass_utils import run_bass_kernel_spmd

F32 = mybir.dt.float32
BF16 = mybir.dt.bfloat16
I32 = mybir.dt.int32
AF = mybir.ActivationFunctionType
ALU = mybir.AluOpType
P = 128

FULL_CFG = dict(B=2, SEQ=4096, DM=2048, HID=2048, HEADS=4, DF=8192,
                LN_EPS=1e-5, RMS_EPS=1e-6, NITER_BIS=36, debug=False)


class KB:

    def __init__(self):
        nc = bass.Bass("TRN2", target_bir_lowering=False,
                       detect_race_conditions=bool(os.environ.get('KRACE')),
                       dynamic_dma_scratch_size=8192)
        self.nc = nc
        from contextlib import ExitStack
        self.es = ExitStack()
        self.ops = {"t": [], "v": [], "s": [], "g": [], "y": []}
        self.sem = {}
        self.cnt = {}

    def semaphore(self, name):
        self.sem[name] = self.es.enter_context(self.nc.semaphore(name))
        self.cnt[name] = 0

    def t(self, fn):
        self.ops["t"].append(fn)

    def v(self, fn):
        self.ops["v"].append(fn)

    def s(self, fn):
        self.ops["s"].append(fn)

    def g(self, fn):
        self.ops["g"].append(fn)

    def y(self, fn):
        self.ops["y"].append(fn)

    def inc(self, name, k=1):
        self.cnt[name] += k
        return self.sem[name], self.cnt[name]

    def val(self, name):
        return self.cnt[name]

    def wait(self, eng_key, name, value):
        if value <= 0:
            return
        sem = self.sem[name]
        self.ops[eng_key].append(lambda e, s=sem, vv=value: e.wait_ge(s, vv))

    def emit(self):
        nc = self.nc
        with nc.Block() as block:
            def run(lst):
                def f(eng):
                    for fn in lst:
                        fn(eng)
                return f
            block.tensor(run(self.ops["t"]))
            block.vector(run(self.ops["v"]))
            block.scalar(run(self.ops["s"]))
            block.gpsimd(run(self.ops["g"]))
            block.sync(run(self.ops["y"]))
        self.es.close()
        return nc


def build(cfg):
    B, SEQ, DM, HID, HEADS, DF = (cfg[k] for k in
                                  ("B", "SEQ", "DM", "HID", "HEADS", "DF"))
    LN_EPS, RMS_EPS = cfg["LN_EPS"], cfg["RMS_EPS"]
    NITER = cfg["NITER_BIS"]
    TOKC = SEQ // 4
    NTC = TOKC // P
    NTH = NTC // 2
    TOKH = TOKC // 2
    NMM = TOKH
    DMC, HIDC, DFC = DM // P, HID // P, DF // P
    HD = HID // HEADS
    HDC = HD // P
    GC = 4 * HID // P
    KSEL = SEQ // 2
    NQ = 4
    DFQ = DF // NQ
    DFQC = DFQ // P
    MG = min(512, HID)
    assert HID % MG == 0 and DM % MG == 0 and HD % P == 0 and DFQ % MG == 0
    assert NMM <= 512 and TOKC % P == 0 and NTC % 2 == 0

    kb = KB()
    nc = kb.nc

    dx = nc.dram_tensor("x_own", [TOKC, DM], F32, kind="ExternalInput").ap()
    dwin = nc.dram_tensor("winT", [DM, HID], F32, kind="ExternalInput").ap()
    dwg = nc.dram_tensor("wgT", [HID, 4 * HID], F32, kind="ExternalInput").ap()
    dwout = nc.dram_tensor("woutT", [HID, DM], F32, kind="ExternalInput").ap()
    dw1 = nc.dram_tensor("w1", [DM, DF], F32, kind="ExternalInput").ap()
    dw2 = nc.dram_tensor("w2", [DF, DM], F32, kind="ExternalInput").ap()
    dbgt = nc.dram_tensor("bg", [P, GC], F32, kind="ExternalInput").ap()
    db1 = nc.dram_tensor("b1c", [P, DFC], F32, kind="ExternalInput").ap()
    db2 = nc.dram_tensor("b2c", [P, DMC], F32, kind="ExternalInput").ap()
    drms = nc.dram_tensor("rmsc", [P, HDC], F32, kind="ExternalInput").ap()
    dwr = nc.dram_tensor("wrb", [P, DM], F32, kind="ExternalInput").ap()
    dbr = nc.dram_tensor("brb", [P, 1], F32, kind="ExternalInput").ap()
    dmq = nc.dram_tensor("mq", [P, 3], F32, kind="ExternalInput").ap()
    didf = nc.dram_tensor("identf", [P, P], F32, kind="ExternalInput").ap()
    dones = nc.dram_tensor("onespp", [P, P], F32, kind="ExternalInput").ap()
    deps = nc.dram_tensor("epsc", [P, 2], F32, kind="ExternalInput").ap()
    dout = nc.dram_tensor("out", [TOKC, DM], F32, kind="ExternalOutput").ap()

    xn_hbm = nc.dram_tensor("xn_hbm", [P, NTC * DM], BF16)
    h_hbm = nc.dram_tensor("h_hbm", [P, HIDC * TOKC], BF16,
                           kind="ExternalOutput" if os.environ.get("KHOUT") else "Internal")
    s_hbm = nc.dram_tensor("s_hbm", [P, DMC * TOKC], BF16)
    sc_in = nc.dram_tensor("sc_in", [P, NTC], F32)
    sc_out = nc.dram_tensor("sc_out", [4 * P, NTC], F32)
    ag_in = [nc.dram_tensor(f"agi{p}", [P, 2 * HDC], F32) for p in range(HEADS)]
    ag_out = [nc.dram_tensor(f"ago{p}", [4 * P, 2 * HDC], F32)
              for p in range(HEADS)]

    es = kb.es
    XT = 0
    XSCR = NTC * DM
    XST = XSCR + DM
    X_END = XST + 8 * NTC
    RF = 0
    RG = HDC * TOKC
    RIO = 2 * HDC * TOKC
    RC = 3 * HDC * TOKC
    RSA = 4 * HDC * TOKC
    RSB = RSA + TOKC
    RZ = RSB + TOKC
    RU = RZ + TOKC
    RTH = RU + NMM
    RSD = RTH + NMM
    RRS = RSD + NMM
    RCSQ = RU
    R_END = RRS + 2 * NMM
    FO = 0
    FU = DMC * NMM
    FGS = FU + NMM
    FTP = FGS + NMM
    FXJ = FTP + 2 * P
    FOJ = FXJ + 2 * DM
    F_END = FOJ + 2 * DM
    WRB = max(X_END, F_END)
    WRB_END = WRB + DM
    MAIN_END = max(WRB_END, R_END)
    PT = MAIN_END
    SC_ = PT + 0
    MSK = PT + 32
    BLO, BHI, BMID = PT + 64, PT + 65, PT + 66
    BD1, BD2, BCNT = PT + 71, PT + 72, PT + 73
    T1_, T2_, T3_ = PT + 68, PT + 69, PT + 70
    AGS = PT + 96
    AGT = PT + 128
    CMPS = PT + 160
    RCNT = PT + 192
    CIN = PT + 196
    SUMM = PT + 208
    STG = PT + 224
    N32 = STG + 1024
    assert NTC <= 32 and 4 * NTC <= 32 and 2 * HDC <= 16
    a32 = es.enter_context(nc.sbuf_tensor("a32", [P, N32], F32))

    def v32(off, cols):
        return a32[:, off:off + cols]

    XN = 0
    XP = XN + NTC * DM
    H1 = XP
    WS0 = XP + HIDC * TOKC
    SLOTW = max(HIDC * MG, DFQC * P)
    WS1 = WS0 + SLOTW
    HS = WS1 + SLOTW
    HSZ = max(HDC * TOKC, (MG // P) * TOKC, DMC * NMM)
    SSL = HS + HSZ
    N16 = SSL + 2 * TOKH + 32
    a16 = es.enter_context(nc.sbuf_tensor("a16", [P, N16], BF16))

    def v16(off, cols):
        return a16[:, off:off + cols]

    pred = es.enter_context(nc.sbuf_tensor("pred", [P, 1], I32))
    cbr = es.enter_context(nc.sbuf_tensor("cbr", [P, 1], F32))
    cbg = es.enter_context(nc.sbuf_tensor("cbg", [P, GC], F32))
    cbgh = es.enter_context(nc.sbuf_tensor("cbgh", [P, GC], F32))
    cb1 = es.enter_context(nc.sbuf_tensor("cb1", [P, DFC], F32))
    cb2 = es.enter_context(nc.sbuf_tensor("cb2", [P, DMC], F32))
    crms = es.enter_context(nc.sbuf_tensor("crms", [P, HDC], F32))
    cmq = es.enter_context(nc.sbuf_tensor("cmq", [P, 3], F32))
    cidf = es.enter_context(nc.sbuf_tensor("cidf", [P, P], F32))
    ceps = es.enter_context(nc.sbuf_tensor("ceps", [P, 2], F32))
    cones = es.enter_context(nc.sbuf_tensor("cones", [P, P], F32))

    ps_a = es.enter_context(nc.psum_tensor("ps_a", [P, 1024], F32))
    ps_b = es.enter_context(nc.psum_tensor("ps_b", [P, 1024], F32))
    ps_c1 = es.enter_context(nc.psum_tensor("ps_c1", [P, 512], F32))
    ps_m = es.enter_context(nc.psum_tensor("ps_m", [P, 512], F32))
    ps_c2 = es.enter_context(nc.psum_tensor("ps_c2", [P, 1024], F32))

    for nm in ("ld", "w0", "w1s", "ccm", "xnw", "hw", "sw", "xnr", "hr",
               "ssl", "xr", "outw", "ten", "vec", "sca", "gp", "cc", "bar"):
        kb.semaphore(nm)

    BAR_LEVEL = cfg.get("barriers", 0)

    def hard_sync(level=1):
        if BAR_LEVEL < level:
            return
        for ek in ("t", "v", "s", "g"):
            h, _ = kb.inc("bar")
            if ek == "t":
                kb.t(lambda e, hh=h: e.nop(nofuse=True).then_inc(hh, 1))
            elif ek == "v":
                kb.v(lambda e, hh=h: e.engine_nop().then_inc(hh, 1))
            elif ek == "s":
                kb.s(lambda e, hh=h: e.nop(nofuse=True).then_inc(hh, 1))
            else:
                kb.g(lambda e, hh=h: e.nop(nofuse=True).then_inc(hh, 1))
        tgt = kb.val("bar")
        for ek in ("t", "v", "s", "g"):
            kb.wait(ek, "bar", tgt)

    def vd():
        kb.v(lambda e: e.drain())

    def gd():
        kb.g(lambda e: e.drain())

    def sd():
        kb.s(lambda e: e.drain())

    lo = v32(BLO, 1)
    hi = v32(BHI, 1)
    mid = v32(BMID, 1)
    rcnt = v32(RCNT, 1)
    cmps = v32(CMPS, 4 * NTC)
    bis = {"i": 0, "last_ten": 0, "inited": False}

    def drip_bis(n=1):
        for _ in range(n):
            i = bis["i"]
            if i >= NITER:
                return
            bis["i"] += 1
            if not bis["inited"]:
                bis["inited"] = True
                kb.wait("v", "ccm", CCM_SC2)
                kb.v(lambda e: e.memset(lo[:, :], -10000.0))
                kb.v(lambda e: e.memset(hi[:, :], 10000.0))
                vd()
            if i > 0:
                kb.wait("v", "ten", bis["last_ten"])
                kb.v(lambda e: e.tensor_scalar(
                    out=v32(BD1, 1), in0=ps_m[:, 0:1], scalar1=float(KSEL),
                    scalar2=None, op0=ALU.is_ge))
                kb.v(lambda e: e.tensor_tensor(out=v32(BD2, 1), in0=mid[:, :],
                                               in1=lo[:, :], op=ALU.subtract))
                vd()
                kb.v(lambda e: e.scalar_tensor_tensor(
                    out=lo[:, :], in0=v32(BD2, 1), scalar=v32(BD1, 1)[:, 0:1],
                    in1=lo[:, :], op0=ALU.mult, op1=ALU.add))
                vd()
                kb.v(lambda e: e.tensor_scalar(
                    out=v32(BD1, 1), in0=ps_m[:, 0:1], scalar1=float(KSEL),
                    scalar2=None, op0=ALU.is_lt))
                kb.v(lambda e: e.tensor_tensor(out=v32(BD2, 1), in0=mid[:, :],
                                               in1=hi[:, :], op=ALU.subtract))
                vd()
                kb.v(lambda e: e.scalar_tensor_tensor(
                    out=hi[:, :], in0=v32(BD2, 1), scalar=v32(BD1, 1)[:, 0:1],
                    in1=hi[:, :], op0=ALU.mult, op1=ALU.add))
                vd()

            kb.v(lambda e: e.tensor_tensor(out=mid[:, :], in0=lo[:, :],
                                           in1=hi[:, :], op=ALU.add))
            vd()
            kb.v(lambda e: e.tensor_scalar(out=mid[:, :], in0=mid[:, :],
                                           scalar1=0.5, scalar2=None,
                                           op0=ALU.mult))
            vd()
            h, _ = kb.inc("vec")
            kb.v(lambda e, hh=h: e.tensor_scalar(
                out=cmps, in0=v32(AGS, 4 * NTC), scalar1=mid[:, 0:1],
                scalar2=None, op0=ALU.is_ge, op1=ALU.add,
                accum_out=rcnt).then_inc(hh, 1))
            kb.wait("t", "vec", kb.val("vec"))
            h2, _ = kb.inc("ten")
            kb.t(lambda e, hh=h2: e.matmul(ps_m[:, 0:1], cones[:, :], rcnt,
                                           start=True, stop=True).then_inc(hh, 1))
            bis["last_ten"] = kb.val("ten")

    sched = []
    slot_state = {"next": 0}
    done_loads = set()
    done_val = {}
    load_cast_val = {}
    load_left = {}
    stg_state = {"j": 0}
    stg_cast_val = {}
    pieceq = []
    WSEMS = ("w0", "w1s")

    def sched_load(src_ap, cols):
        lid = len(sched)
        kk = src_ap.shape[0] // P
        base = (WS0, WS1)[lid % 2]
        npc = 0
        if cols >= 512:
            pw = 512
            for k in range(kk):
                for c0 in range(0, cols, pw):
                    pieceq.append((lid,
                                   src_ap[k * P:(k + 1) * P, c0:c0 + pw],
                                   base + k * cols + c0, pw, npc))
                    npc += 1
        else:
            kb_ = 512 // cols
            for k0 in range(0, kk, kb_):
                nkb = min(kb_, kk - k0)
                src3 = src_ap[k0 * P:(k0 + nkb) * P, :].rearrange(
                    "(k p) n -> p k n", p=P)
                pieceq.append((lid, src3, base + k0 * cols,
                               nkb * cols, npc))
                npc += 1
        sched.append(cols)
        load_left[lid] = npc

    def drip_piece(n=1):
        for _ in range(n):
            if not pieceq:
                return
            lid, src, dst_off, pw, pidx = pieceq[0]
            if lid >= 2 and (lid - 2) not in done_loads:
                return
            pieceq.pop(0)
            j = stg_state["j"]
            stg_state["j"] += 1
            sem = WSEMS[j % 2]
            ce = ("s", "v")[j % 2]
            ctr = ("sca", "vec")[j % 2]
            if j >= 2:
                kb.wait("s", ("sca", "vec")[j % 2], stg_cast_val[j - 2])
            h, _ = kb.inc(sem, 16)
            dstg = v32(STG + (j % 2) * 512, pw)
            if len(src.shape) == 3:
                dstg = dstg.rearrange("p (k n) -> p k n", k=src.shape[1])
            kb.s(lambda e, d=dstg, s2=src, hh=h:
                 e.dma_start(out=d, in_=s2).then_inc(hh, 16))
            if pidx < 2 and lid >= 2:
                kb.wait(ce, "ten", done_val[lid - 2])
            kb.wait(ce, sem, kb.val(sem))
            h2, _ = kb.inc(ctr)
            cast = (lambda e, d=v16(dst_off, pw),
                    s2=v32(STG + (j % 2) * 512, pw), hh=h2:
                    e.activation(out=d, in_=s2,
                                 func=AF.Copy).then_inc(hh, 1)) \
                if ce == "s" else \
                (lambda e, d=v16(dst_off, pw),
                 s2=v32(STG + (j % 2) * 512, pw), hh=h2:
                 e.tensor_copy(out=d, in_=s2).then_inc(hh, 1))
            (kb.s if ce == "s" else kb.v)(cast)
            stg_cast_val[j] = kb.val(ctr)
            load_cast_val.setdefault(lid, {})[ctr] = kb.val(ctr)
            load_left[lid] -= 1

    def next_slot(cols):
        lid = slot_state["next"]
        slot_state["next"] += 1
        assert sched[lid] == cols
        while load_left[lid]:
            drip_piece()
        drip_bis(1)
        base = (WS0, WS1)[lid % 2]

        def view(k, c0, c1, base=base, cols=cols):
            return v16(base + k * cols + c0, c1 - c0)
        return lid, view, list(load_cast_val[lid].items())

    def slot_done(lid):
        done_loads.add(lid)
        done_val[lid] = kb.val("ten")

    for m_ in range(HID // MG):
        sched_load(dwin[:, m_ * MG:(m_ + 1) * MG], MG)
    for p2_ in range(HEADS):
        for ty_ in (0, 3, 1, 2):
            c0_ = p2_ * 4 * HD + ty_ * HD
            sched_load(dwg[:, c0_:c0_ + HD], HD)
    for m_ in range(DM // MG):
        sched_load(dwout[:, m_ * MG:(m_ + 1) * MG], MG)
    for th_ in range(2):
        for qq_ in range(NQ):
            for mg_ in range(DFQ // MG):
                gc_ = qq_ * DFQ + mg_ * MG
                sched_load(dw1[:, gc_:gc_ + MG], MG)
            for c_ in range(DMC):
                sched_load(dw2[qq_ * DFQ:(qq_ + 1) * DFQ,
                               c_ * P:(c_ + 1) * P], P)

    rot = {"i": 0, "hist": []}

    def mm_group(nchunks, nk, lhsT_fn, rhs_fn, evac_fn, evac_eng, pre_wait):
        gi = rot["i"]
        rot["i"] += 1
        buf = (ps_a, ps_b)[gi % 2]
        nloc = rhs_fn(0).shape[-1]
        if gi >= 2:
            nm, vvv = rot["hist"][gi - 2]
            kb.wait("t", nm, vvv)
        if pre_wait is not None:
            pws = pre_wait if isinstance(pre_wait, list) else [pre_wait]
            for nm, vvv in pws:
                kb.wait("t", nm, vvv)
        for k in range(nk):
            for c in range(nchunks):
                pslice = buf[:, c * 512:c * 512 + nloc]
                st, sp = (k == 0), (k == nk - 1)
                if sp and c == nchunks - 1:
                    h, _ = kb.inc("ten")
                    kb.t(lambda e, p2=pslice, l=lhsT_fn(k, c), r=rhs_fn(k),
                         s1=st, s2=sp, hh=h:
                         e.matmul(p2, l, r, start=s1, stop=s2).then_inc(hh, 1))
                else:
                    kb.t(lambda e, p2=pslice, l=lhsT_fn(k, c), r=rhs_fn(k),
                         s1=st, s2=sp: e.matmul(p2, l, r, start=s1, stop=s2))
        mm_val = kb.val("ten")
        kb.wait(evac_eng, "ten", mm_val)
        for c in range(nchunks):
            evac_fn(c, buf[:, c * 512:c * 512 + nloc])
        ctr = {"v": "vec", "s": "sca"}[evac_eng]
        rot["hist"].append((ctr, kb.val(ctr)))
        drip_piece(4)

    def y_dma(dst, src, sem="ld"):
        h, val = kb.inc(sem, 16)
        kb.y(lambda e, d=dst, s=src, hh=h:
             e.dma_start(out=d, in_=s).then_inc(hh, 16))
        return val

    y_dma(v32(XT, NTC * DM).rearrange("p (j d) -> p j d", j=NTC),
          dx.rearrange("(j p) d -> p j d", p=P))
    y_dma(v32(WRB, DM), dwr)
    y_dma(cbr[:, :], dbr)
    y_dma(cbg[:, :], dbgt)
    y_dma(cb1[:, :], db1)
    y_dma(cb2[:, :], db2)
    y_dma(crms[:, :], drms)
    y_dma(cmq[:, :], dmq)
    y_dma(cidf[:, :], didf)
    y_dma(cones[:, :], dones)
    y_dma(ceps[:, :], deps)
    LD_ALL = kb.val("ld")

    st_sum = v32(XST + 0 * NTC, NTC)
    st_m = v32(XST + 1 * NTC, NTC)
    st_var = v32(XST + 2 * NTC, NTC)
    st_sd = v32(XST + 3 * NTC, NTC)
    st_rs = v32(XST + 4 * NTC, NTC)
    st_nm = v32(XST + 5 * NTC, NTC)
    st_sq = v32(XST + 6 * NTC, NTC)
    st_t = v32(XST + 7 * NTC, NTC)

    kb.wait("v", "ld", LD_ALL)
    kb.wait("s", "ld", LD_ALL)
    for j in range(NTC):
        kb.v(lambda e, j=j: e.reduce_sum(
            out=v32(XST, NTC)[:, j:j + 1], in_=v32(XT + j * DM, DM),
            axis=mybir.AxisListType.X))
        h, _ = kb.inc("sca")
        kb.s(lambda e, j=j, hh=h: e.activation(
            out=v32(XSCR, DM), in_=v32(XT + j * DM, DM), func=AF.Square,
            accum_out=v32(XST + 6 * NTC, NTC)[:, j:j + 1]).then_inc(hh, 1))
        sd()
    SQ_ALL = kb.val("sca")
    kb.wait("v", "sca", SQ_ALL)
    vd()
    kb.v(lambda e: e.tensor_scalar(out=st_m, in0=st_sum, scalar1=1.0 / DM,
                                   scalar2=None, op0=ALU.mult))
    vd()
    kb.v(lambda e: e.tensor_scalar(out=st_var, in0=st_sq, scalar1=1.0 / DM,
                                   scalar2=None, op0=ALU.mult))
    vd()
    kb.v(lambda e: e.scalar_tensor_tensor(out=st_t, in0=st_m, scalar=-1.0,
                                          in1=st_m, op0=ALU.mult, op1=ALU.mult))
    vd()
    h, _ = kb.inc("vec")
    kb.v(lambda e, hh=h: e.tensor_tensor(out=st_var, in0=st_var, in1=st_t,
                                         op=ALU.add).then_inc(hh, 1))
    kb.wait("s", "vec", kb.val("vec"))
    h, _ = kb.inc("sca")
    kb.s(lambda e, hh=h: e.activation(out=st_sd, in_=st_var, func=AF.Sqrt,
                                      bias=ceps[:, 0:1]).then_inc(hh, 1))
    kb.wait("v", "sca", kb.val("sca"))
    kb.v(lambda e: e.reciprocal(out=st_rs, in_=st_sd))
    vd()
    kb.v(lambda e: e.scalar_tensor_tensor(out=st_nm, in0=st_m, scalar=-1.0,
                                          in1=st_rs, op0=ALU.mult, op1=ALU.mult))
    vd()
    for j in range(NTC):
        kb.v(lambda e, j=j: e.tensor_scalar(
            out=v32(XT + j * DM, DM), in0=v32(XT + j * DM, DM),
            scalar1=v32(XST + 4 * NTC, NTC)[:, j:j + 1],
            scalar2=v32(XST + 5 * NTC, NTC)[:, j:j + 1],
            op0=ALU.mult, op1=ALU.add))
    vd()
    h, _ = kb.inc("vec")
    kb.v(lambda e, hh=h: e.engine_nop().then_inc(hh, 1))
    V_XN = kb.val("vec")
    for j in range(NTC):
        kb.v(lambda e, j=j: e.tensor_tensor(out=v32(XSCR, DM),
                                            in0=v32(XT + j * DM, DM),
                                            in1=v32(WRB, DM), op=ALU.mult))
        vd()
        kb.v(lambda e, j=j: e.reduce_sum(out=v32(SC_, NTC)[:, j:j + 1],
                                         in_=v32(XSCR, DM),
                                         axis=mybir.AxisListType.X))
        vd()
    h, _ = kb.inc("vec")
    kb.v(lambda e, hh=h: e.tensor_scalar(out=v32(SC_, NTC), in0=v32(SC_, NTC),
                                         scalar1=cbr[:, 0:1], scalar2=None,
                                         op0=ALU.add).then_inc(hh, 1))
    V_SC = kb.val("vec")
    kb.v(lambda e: e.tensor_scalar(out=cbgh[:, :], in0=cbg[:, :], scalar1=0.5,
                                   scalar2=None, op0=ALU.mult))
    kb.v(lambda e: e.memset(v32(RZ, TOKC), 0.0))
    h, _ = kb.inc("vec")
    kb.v(lambda e, hh=h: e.engine_nop().then_inc(hh, 1))
    V_PREP = kb.val("vec")

    kb.wait("y", "vec", V_SC)
    kb.wait("y", "ccm", kb.val("ccm"))
    h, _ = kb.inc("ccm", 16)
    kb.y(lambda e, hh=h: e.dma_start(out=sc_in.ap(),
                                     in_=v32(SC_, NTC)).then_inc(hh, 16))
    kb.wait("g", "ccm", kb.val("ccm"))
    h, _ = kb.inc("cc")
    kb.g(lambda e, hh=h: e.collective_compute(
        "AllGather", ALU.bypass, replica_groups=[[0, 1, 2, 3], [4, 5, 6, 7]],
        ins=[sc_in.ap().opt()], outs=[sc_out.ap().opt()]).then_inc(hh, 1))
    kb.wait("y", "cc", kb.val("cc"))
    kb.wait("y", "ccm", kb.val("ccm"))
    h, _ = kb.inc("ccm", 16)
    kb.y(lambda e, hh=h: e.dma_start(
        out=v32(AGS, 4 * NTC).rearrange("p (c j) -> p c j", c=4),
        in_=sc_out.ap().rearrange("(c p) j -> p c j", p=P)).then_inc(hh, 16))
    CCM_SC2 = kb.val("ccm")

    kb.wait("t", "vec", V_XN)
    TP_SLOTS = [ps_a[:, 0:P], ps_a[:, 512:512 + P],
                ps_b[:, 0:P], ps_b[:, 512:512 + P]]
    S_TPBASE = kb.val("sca")
    n_tp = 0
    for j in range(NTC):
        for d in range(DMC):
            tpi = n_tp
            n_tp += 1
            slot = TP_SLOTS[tpi % 4]
            if tpi >= 4:
                kb.wait("t", "sca", S_TPBASE + tpi - 3)
            h, _ = kb.inc("ten")
            kb.t(lambda e, sl=slot, j=j, d=d, hh=h: e.transpose(
                sl, v32(XT + j * DM + d * P, P), cidf[:, :]).then_inc(hh, 1))
            kb.wait("s", "ten", kb.val("ten"))
            h, _ = kb.inc("sca")
            kb.s(lambda e, sl=slot, j=j, d=d, hh=h: e.activation(
                out=v16(XN + d * TOKC + j * P, P), in_=sl,
                func=AF.Copy).then_inc(hh, 1))
    S_XNFM = kb.val("sca")
    kb.wait("s", "sca", S_XNFM)
    h, _ = kb.inc("xnw", 16)
    kb.s(lambda e, hh=h: e.dma_start(out=xn_hbm.ap(),
                                     in_=v16(XN, NTC * DM)).then_inc(hh, 16))

    kb.wait("t", "sca", S_XNFM)
    for m in range(HID // MG):
        si, wv, wguard = next_slot(MG)
        nch = MG // P
        for cp in range(0, nch, 2):
            ncp = min(2, nch - cp)
            for n in range(2):
                def evac(c, pslice, m=m, cp=cp, n=n, nch=nch):
                    ch = m * nch + cp + c
                    h, _ = kb.inc("sca")
                    kb.s(lambda e, p2=pslice, ch=ch, n=n, hh=h: e.activation(
                        out=v16(XP + ch * TOKC + n * NMM, NMM), in_=p2,
                        func=AF.Copy).then_inc(hh, 1))
                mm_group(ncp, DMC,
                         lambda k, c, wv=wv, cp=cp: wv(k, (cp + c) * P,
                                                       (cp + c + 1) * P),
                         lambda k, n=n: v16(XN + k * TOKC + n * NMM, NMM),
                         evac, "s", pre_wait=list(wguard))
        slot_done(si)
        hard_sync(2)
    S_XP = kb.val("sca")

    def fsl(ct, c0, n):
        return v32(RF + ct * TOKC + c0, n)

    def gsl(ct, c0, n):
        return v32(RG + ct * TOKC + c0, n)

    def iosl(ct, c0, n):
        return v32(RIO + ct * TOKC + c0, n)

    def csl(ct, c0, n):
        return v32(RC + ct * TOKC + c0, n)

    V_LASTCSCAN = 0
    last_hw = 0
    for p_ in range(HEADS):
        for ty in (0, 3, 1, 2):
            col0 = p_ * 4 * HD + ty * HD
            si, wv, wguard = next_slot(HD)
            if ty == 2:
                VG = kb.val("vec")
            for cp in range(0, HDC, 2):
                ncp = min(2, HDC - cp)
                for n in range(2):
                    first = (cp == 0 and n == 0)

                    def evac(c, pslice, ty=ty, cp=cp, n=n, p_=p_, first=first):
                        ct = cp + c
                        gidx = (p_ * 4 * HD + ty * HD) // P + ct
                        if ty == 0 and first and c == 0:
                            kb.wait("s", "vec", V_LASTCSCAN)
                        if ty == 2 and first and c == 0:
                            kb.wait("s", "vec", VG)
                        if ty == 0:
                            h, _ = kb.inc("sca")
                            kb.s(lambda e, p2=pslice, d=iosl(ct, n * NMM, NMM),
                                 g2=gidx, hh=h: e.activation(
                                     out=d, in_=p2, func=AF.Exp,
                                     bias=cbg[:, g2:g2 + 1]).then_inc(hh, 1))
                        elif ty == 3:
                            h, _ = kb.inc("sca")
                            kb.s(lambda e, p2=pslice, d=gsl(ct, n * NMM, NMM),
                                 g2=gidx, hh=h: e.activation(
                                     out=d, in_=p2, func=AF.Tanh,
                                     bias=cbg[:, g2:g2 + 1]).then_inc(hh, 1))
                            kb.wait("v", "sca", kb.val("sca"))
                            h2, _ = kb.inc("vec")
                            kb.v(lambda e, ct=ct, n=n, hh=h2: e.tensor_tensor(
                                out=gsl(ct, n * NMM, NMM),
                                in0=iosl(ct, n * NMM, NMM),
                                in1=gsl(ct, n * NMM, NMM),
                                op=ALU.mult).then_inc(hh, 1))
                        else:
                            dst = (fsl if ty == 1 else iosl)(ct, n * NMM, NMM)
                            h, _ = kb.inc("sca")
                            kb.s(lambda e, p2=pslice, d=dst, g2=gidx, hh=h:
                                 e.activation(out=d, in_=p2, func=AF.Tanh,
                                              scale=0.5,
                                              bias=cbgh[:, g2:g2 + 1]).then_inc(hh, 1))
                            kb.wait("v", "sca", kb.val("sca"))
                            h2, _ = kb.inc("vec")
                            kb.v(lambda e, d=dst, hh=h2: e.tensor_scalar(
                                out=d, in0=d, scalar1=0.5, scalar2=0.5,
                                op0=ALU.mult, op1=ALU.add).then_inc(hh, 1))
                    pw = list(wguard)
                    if p_ == 0 and ty == 0 and cp == 0 and n == 0:
                        pw.append(("sca", S_XP))
                    mm_group(ncp, HIDC,
                             lambda k, c, wv=wv, cp=cp: wv(k, (cp + c) * P,
                                                           (cp + c + 1) * P),
                             lambda k, n=n: v16(XP + k * TOKC + n * NMM, NMM),
                             evac, "s", pre_wait=pw)
            slot_done(si)
            hard_sync(2)
        S_GATES = kb.val("sca")
        V_GATES = kb.val("vec")

        kb.wait("v", "sca", S_GATES)
        drip_piece(4)
        for ct in range(HDC):
            kb.v(lambda e, ct=ct: e.tensor_tensor_scan(
                out=v32(RSA, TOKC), data0=fsl(ct, 0, TOKC),
                data1=gsl(ct, 0, TOKC), initial=0.0,
                op0=ALU.mult, op1=ALU.add))
            vd()
            kb.v(lambda e, ct=ct: e.tensor_copy(
                out=v32(SUMM + ct, 1), in_=v32(RSA + TOKC - 1, 1)))
            kb.v(lambda e, ct=ct: e.tensor_tensor_scan(
                out=v32(RSB, TOKC), data0=fsl(ct, 0, TOKC),
                data1=v32(RZ, TOKC), initial=1.0,
                op0=ALU.mult, op1=ALU.add))
            vd()
            kb.v(lambda e, ct=ct: e.tensor_copy(
                out=v32(SUMM + HDC + ct, 1), in_=v32(RSB + TOKC - 1, 1)))
            drip_piece(2)
        h, _ = kb.inc("vec")
        kb.v(lambda e, hh=h: e.engine_nop().then_inc(hh, 1))
        V_SUMM = kb.val("vec")
        hard_sync(2)

        kb.wait("y", "vec", V_SUMM)
        kb.wait("y", "ccm", kb.val("ccm"))
        h, _ = kb.inc("ccm", 16)
        kb.y(lambda e, p_=p_, hh=h: e.dma_start(
            out=ag_in[p_].ap(), in_=v32(SUMM, 2 * HDC)).then_inc(hh, 16))
        kb.wait("g", "ccm", kb.val("ccm"))
        h, _ = kb.inc("cc")
        kb.g(lambda e, p_=p_, hh=h: e.collective_compute(
            "AllGather", ALU.bypass,
            replica_groups=[[0, 1, 2, 3], [4, 5, 6, 7]],
            ins=[ag_in[p_].ap().opt()],
            outs=[ag_out[p_].ap().opt()]).then_inc(hh, 1))
        kb.wait("y", "cc", kb.val("cc"))
        kb.wait("y", "ccm", kb.val("ccm"))
        h, _ = kb.inc("ccm", 16)
        kb.y(lambda e, p_=p_, hh=h: e.dma_start(
            out=v32(AGT, 4 * 2 * HDC).rearrange("p (c j) -> p c j", c=4),
            in_=ag_out[p_].ap().rearrange("(c p) j -> p c j", p=P)).then_inc(hh, 16))
        CCM_SUM = kb.val("ccm")

        if cfg.get("debug"):
            if p_ == 0:
                dbg_ag = nc.dram_tensor("dbg_ag", [P, HEADS * 8 * HDC], F32,
                                        kind="ExternalOutput")
                build.dbg_ag = dbg_ag
            kb.wait("y", "ccm", CCM_SUM)
            kb.wait("y", "ccm", kb.val("ccm"))
            h, _ = kb.inc("ccm", 16)
            kb.y(lambda e, p_=p_, hh=h: e.dma_start(
                out=build.dbg_ag.ap()[:, p_ * 8 * HDC:(p_ + 1) * 8 * HDC],
                in_=v32(AGT, 4 * 2 * HDC)).then_inc(hh, 16))
            CCM_SUM = kb.val("ccm")
        kb.wait("v", "ccm", CCM_SUM)
        W = 2 * HDC
        t1, t2, t3 = v32(T1_, 1), v32(T2_, 1), v32(T3_, 1)
        for ct in range(HDC):
            s0 = v32(AGT + 0 * W + ct, 1)
            s1_ = v32(AGT + 1 * W + ct, 1)
            s2_ = v32(AGT + 2 * W + ct, 1)
            p1 = v32(AGT + 1 * W + HDC + ct, 1)
            p2_ = v32(AGT + 2 * W + HDC + ct, 1)
            cin = v32(CIN + ct, 1)
            kb.v(lambda e, a=t1, b=s0: e.tensor_copy(out=a, in_=b))
            vd()
            kb.v(lambda e, a=t2, b=p1, c=t1, d=s1_: e.scalar_tensor_tensor(
                out=a, in0=b, scalar=c[:, 0:1], in1=d,
                op0=ALU.mult, op1=ALU.add))
            vd()
            kb.v(lambda e, a=t3, b=p2_, c=t2, d=s2_: e.scalar_tensor_tensor(
                out=a, in0=b, scalar=c[:, 0:1], in1=d,
                op0=ALU.mult, op1=ALU.add))
            vd()
            kb.v(lambda e, a=cin, b=t1: e.tensor_tensor(
                out=a, in0=b, in1=cmq[:, 0:1], op=ALU.mult))
            vd()
            kb.v(lambda e, a=cin, b=t2: e.scalar_tensor_tensor(
                out=a, in0=b, scalar=cmq[:, 1:2], in1=a,
                op0=ALU.mult, op1=ALU.add))
            vd()
            kb.v(lambda e, a=cin, b=t3: e.scalar_tensor_tensor(
                out=a, in0=b, scalar=cmq[:, 2:3], in1=a,
                op0=ALU.mult, op1=ALU.add))
            vd()
            kb.v(lambda e, ct=ct, a=cin: e.tensor_tensor_scan(
                out=csl(ct, 0, TOKC), data0=fsl(ct, 0, TOKC),
                data1=gsl(ct, 0, TOKC), initial=a[:, 0:1],
                op0=ALU.mult, op1=ALU.add))
            vd()
            drip_piece(2)
        h, _ = kb.inc("vec")
        kb.v(lambda e, hh=h: e.engine_nop().then_inc(hh, 1))
        V_LASTCSCAN = kb.val("vec")
        hard_sync(2)

        kb.wait("s", "vec", V_LASTCSCAN)
        kb.wait("v", "hw", last_hw)
        for n in range(2):
            if n:
                kb.wait("s", "vec", kb.val("vec"))
            for ct in range(HDC):
                h, _ = kb.inc("sca")
                kb.s(lambda e, ct=ct, n=n, hh=h: e.activation(
                    out=v32(RCSQ + (ct % 2) * NMM, NMM),
                    in_=csl(ct, n * NMM, NMM),
                    func=AF.Square).then_inc(hh, 1))
                kb.wait("t", "sca", kb.val("sca"))
                h2, _ = kb.inc("ten")
                kb.t(lambda e, ct=ct, hh=h2: e.matmul(
                    ps_c1[:, 0:NMM], cones[:, :],
                    v32(RCSQ + (ct % 2) * NMM, NMM),
                    start=(ct == 0), stop=(ct == HDC - 1)).then_inc(hh, 1))
                if ct + 2 < HDC:
                    kb.wait("s", "ten", kb.val("ten"))
            kb.wait("s", "ten", kb.val("ten"))
            h, _ = kb.inc("sca")
            kb.s(lambda e, hh=h: e.activation(
                out=v32(RSD, NMM), in_=ps_c1[:, 0:NMM], func=AF.Sqrt,
                scale=1.0 / HD, bias=ceps[:, 1:2]).then_inc(hh, 1))
            kb.wait("v", "sca", kb.val("sca"))
            h, _ = kb.inc("vec")
            kb.v(lambda e, n=n, hh=h: e.reciprocal(
                out=v32(RRS + n * NMM, NMM),
                in_=v32(RSD, NMM)).then_inc(hh, 1))
            vd()
            kb.wait("t", "vec", kb.val("vec"))
            for ct in range(HDC):
                h, _ = kb.inc("vec")
                kb.v(lambda e, ct=ct, n=n, hh=h: e.scalar_tensor_tensor(
                    out=v32(RU, NMM), in0=csl(ct, n * NMM, NMM),
                    scalar=crms[:, ct:ct + 1], in1=v32(RRS + n * NMM, NMM),
                    op0=ALU.mult, op1=ALU.mult).then_inc(hh, 1))
                kb.wait("s", "vec", kb.val("vec"))
                h, _ = kb.inc("sca")
                kb.s(lambda e, hh=h: e.activation(
                    out=v32(RTH, NMM), in_=v32(RU, NMM),
                    func=AF.Tanh).then_inc(hh, 1))
                kb.wait("v", "sca", kb.val("sca"))
                h, _ = kb.inc("vec")
                kb.v(lambda e, ct=ct, n=n, hh=h: e.tensor_tensor(
                    out=v16(HS + ct * TOKC + n * NMM, NMM),
                    in0=iosl(ct, n * NMM, NMM),
                    in1=v32(RTH, NMM), op=ALU.mult).then_inc(hh, 1))
                drip_piece(2)
        h, _ = kb.inc("vec")
        kb.v(lambda e, hh=h: e.engine_nop().then_inc(hh, 1))
        V_H = kb.val("vec")
        V_LASTCSCAN = V_H
        hard_sync(1)
        kb.wait("s", "vec", V_H)
        h, _ = kb.inc("hw", 16)
        kb.s(lambda e, p_=p_, hh=h: e.dma_start(
            out=h_hbm.ap()[:, p_ * HDC * TOKC:(p_ + 1) * HDC * TOKC],
            in_=v16(HS, HDC * TOKC)).then_inc(hh, 16))
        last_hw = kb.val("hw")

    if cfg.get("debug"):
        dbg_r = nc.dram_tensor("dbg_r", [P, 4 * HDC * TOKC], F32,
                               kind="ExternalOutput")
        dbg_xpe = nc.dram_tensor("dbg_xpe", [P, HIDC * TOKC], BF16,
                                 kind="ExternalOutput")
        kb.wait("y", "vec", kb.val("vec"))
        h, _ = kb.inc("outw", 16)
        kb.y(lambda e, hh=h: e.dma_start(out=dbg_r.ap(),
                                         in_=v32(RF, 4 * HDC * TOKC)).then_inc(hh, 16))
        h, _ = kb.inc("outw", 16)
        kb.y(lambda e, hh=h: e.dma_start(out=dbg_xpe.ap(),
                                         in_=v16(XP, HIDC * TOKC)).then_inc(hh, 16))
        DBGR_V = kb.val("outw")
        kb.wait("v", "outw", DBGR_V)
        kb.wait("s", "outw", DBGR_V)
    STOP = cfg.get("stop_after")
    if STOP == "R":
        kb.wait("y", "hw", kb.val("hw"))
        kb.wait("y", "vec", kb.val("vec"))
        return kb.emit()
    drip_bis(NITER)
    kb.wait("v", "ten", bis["last_ten"])
    kb.v(lambda e: e.tensor_scalar(out=v32(BD1, 1), in0=ps_m[:, 0:1],
                                   scalar1=float(KSEL), scalar2=None,
                                   op0=ALU.is_ge))
    kb.v(lambda e: e.tensor_tensor(out=v32(BD2, 1), in0=mid[:, :],
                                   in1=lo[:, :], op=ALU.subtract))
    vd()
    kb.v(lambda e: e.scalar_tensor_tensor(
        out=lo[:, :], in0=v32(BD2, 1), scalar=v32(BD1, 1)[:, 0:1],
        in1=lo[:, :], op0=ALU.mult, op1=ALU.add))
    vd()
    h, _ = kb.inc("vec")
    kb.v(lambda e, hh=h: e.tensor_scalar(
        out=v32(MSK, NTC), in0=v32(SC_, NTC), scalar1=lo[:, 0:1],
        scalar2=None, op0=ALU.is_ge).then_inc(hh, 1))
    GP_MASK = kb.val("vec")
    STRUNC = cfg.get("s_trunc")

    def dump_arenas():
        if not cfg.get("dump_arenas"):
            return
        d16 = nc.dram_tensor("dbg_a16d", [P, N16], BF16,
                             kind="ExternalOutput")
        d32 = nc.dram_tensor("dbg_a32d", [P, N32], F32,
                             kind="ExternalOutput")
        kb.wait("y", "vec", kb.val("vec"))
        kb.wait("y", "sca", kb.val("sca"))
        h2, _ = kb.inc("outw", 16)
        kb.y(lambda e, hh=h2: e.dma_start(out=d16.ap(),
                                          in_=a16[:, :]).then_inc(hh, 16))
        h2, _ = kb.inc("outw", 16)
        kb.y(lambda e, hh=h2: e.dma_start(out=d32.ap(),
                                          in_=a32[:, :]).then_inc(hh, 16))
        kb.wait("y", "outw", kb.val("outw"))

    if STRUNC == 1:
        kb.wait("y", "hw", kb.val("hw"))
        kb.wait("y", "vec", kb.val("vec"))
        dump_arenas()
        return kb.emit()

    kb.wait("y", "hw", last_hw)
    h, _ = kb.inc("hr", 16)
    kb.y(lambda e, hh=h: e.dma_start(out=v16(XP, HIDC * TOKC),
                                     in_=h_hbm.ap()).then_inc(hh, 16))
    if STRUNC == 2:
        kb.wait("y", "hr", kb.val("hr"))
        dump_arenas()
        return kb.emit()
    kb.wait("t", "hr", kb.val("hr"))
    kb.wait("s", "hw", last_hw)
    nch_s = MG // P
    for m in range(DM // MG):
        si, wv, wguard = next_slot(MG)
        for cp in range(0, nch_s, 2):
            ncp = min(2, nch_s - cp)
            for n in range(2):
                def evac(c, pslice, cp=cp, n=n):
                    lc = cp + c
                    h, _ = kb.inc("sca")
                    kb.s(lambda e, p2=pslice, lc=lc, n=n, hh=h: e.activation(
                        out=v16(HS + lc * TOKC + n * NMM, NMM), in_=p2,
                        func=AF.Copy).then_inc(hh, 1))
                mm_group(ncp, HIDC,
                         lambda k, c, wv=wv, cp=cp: wv(k, (cp + c) * P,
                                                       (cp + c + 1) * P),
                         lambda k, n=n: v16(XP + k * TOKC + n * NMM, NMM),
                         evac, "s", pre_wait=list(wguard))
        slot_done(si)
        kb.wait("s", "sca", kb.val("sca"))
        h, _ = kb.inc("sw", 16)
        kb.s(lambda e, m=m, hh=h: e.dma_start(
            out=s_hbm.ap()[:, m * nch_s * TOKC:(m + 1) * nch_s * TOKC],
            in_=v16(HS, nch_s * TOKC)).then_inc(hh, 16))
        kb.wait("s", "sw", kb.val("sw"))

    if STOP == "S":
        kb.wait("y", "sw", kb.val("sw"))
        return kb.emit()
    kb.wait("y", "xnw", 16)
    h, _ = kb.inc("xnr", 16)
    kb.y(lambda e, hh=h: e.dma_start(out=v16(XN, NTC * DM),
                                     in_=xn_hbm.ap()).then_inc(hh, 16))
    kb.wait("t", "xnr", kb.val("xnr"))
    kb.wait("v", "vec", GP_MASK) if False else None
    kb.wait("v", "sw", kb.val("sw"))

    ssl_loaded = {}
    prd_done = {}
    out_done = {}
    fo_ready = {}
    last_silu = {"v": 0}
    out_dma_val = {}
    V_REND = kb.val("vec")
    first_xr = [True]
    xr_cnt = {"n": 0}
    FTP_SLOTS = [ps_c1[:, 0:P], ps_m[:, 0:P]]
    n_ftp = {"n": 0}
    for th in range(2):
        for qq in range(NQ):
            for mg in range(DFQ // MG):
                gcol = qq * DFQ + mg * MG
                si, wv, wguard = next_slot(MG)
                nch = MG // P
                for cp in range(0, nch, 2):
                    ncp = min(2, nch - cp)

                    def evac(c, pslice, mg=mg, cp=cp, qq=qq, nch=nch):
                        lc = mg * nch + cp + c
                        gch = qq * DFQC + lc
                        h, _ = kb.inc("sca")
                        kb.s(lambda e, p2=pslice, lc=lc, gch=gch, hh=h:
                             e.activation(out=v16(H1 + lc * NMM, NMM), in_=p2,
                                          func=AF.Gelu_apprx_tanh,
                                          bias=cb1[:, gch:gch + 1]).then_inc(hh, 1))
                    mm_group(ncp, DMC,
                             lambda k, c, wv=wv, cp=cp: wv(k, (cp + c) * P,
                                                           (cp + c + 1) * P),
                             lambda k, th=th: v16(XN + k * TOKC + th * TOKH,
                                                  TOKH),
                             evac, "s", pre_wait=list(wguard))
                slot_done(si)
            S_H1 = kb.val("sca")
            first_w2 = [True]
            for c in range(DMC):
                si, wv, wguard = next_slot(P)
                if qq == NQ - 1:
                    key = (th, c)
                    kb.wait("y", "sw", kb.val("sw"))
                    kb.wait("y", "ssl", kb.val("ssl"))
                    pk = (th, c - 2) if c >= 2 else (1 - th, DMC - 2 + c)
                    if pk in prd_done:
                        kb.wait("y", "vec", prd_done[pk])
                    h, _ = kb.inc("ssl", 16)
                    kb.y(lambda e, c=c, th=th, hh=h: e.dma_start(
                        out=v16(SSL + (c % 2) * TOKH, TOKH),
                        in_=s_hbm.ap()[:, c * TOKC + th * TOKH:
                                       c * TOKC + th * TOKH + TOKH]).then_inc(hh, 16))
                    ssl_loaded[key] = kb.val("ssl")

                def evac2(cc, pslice, c=c, qq=qq, th=th):
                    if qq == 0:
                        h, _ = kb.inc("vec")
                        kb.v(lambda e, p2=pslice, c=c, hh=h: e.tensor_scalar(
                            out=v32(FO + c * NMM, NMM), in0=p2,
                            scalar1=cb2[:, c:c + 1], scalar2=None,
                            op0=ALU.add).then_inc(hh, 1))
                        fo_ready[(th, c)] = kb.val("vec")
                    elif qq < NQ - 1:
                        if qq == 1:
                            kb.wait("v", "vec", fo_ready[(th, c)])
                        h, _ = kb.inc("vec")
                        kb.v(lambda e, p2=pslice, c=c, hh=h: e.tensor_tensor(
                            out=v32(FO + c * NMM, NMM), in0=p2,
                            in1=v32(FO + c * NMM, NMM),
                            op=ALU.add).then_inc(hh, 1))
                    else:
                        kb.wait("v", "sca", last_silu["v"])
                        h, _ = kb.inc("vec")
                        kb.v(lambda e, p2=pslice, c=c, hh=h: e.tensor_tensor(
                            out=v32(FU, NMM), in0=p2,
                            in1=v32(FO + c * NMM, NMM),
                            op=ALU.add).then_inc(hh, 1))
                        kb.wait("s", "vec", kb.val("vec"))
                        h, _ = kb.inc("sca")
                        kb.s(lambda e, hh=h: e.activation(
                            out=v32(FGS, NMM), in_=v32(FU, NMM),
                            func=AF.Silu).then_inc(hh, 1))
                        last_silu["v"] = kb.val("sca")
                        kb.wait("v", "sca", kb.val("sca"))
                        kb.wait("v", "ssl", ssl_loaded[(th, c)])
                        h, _ = kb.inc("vec")
                        kb.v(lambda e, c=c, hh=h: e.tensor_tensor(
                            out=v16(HS + c * NMM, NMM), in0=v32(FGS, NMM),
                            in1=v16(SSL + (c % 2) * TOKH, TOKH),
                            op=ALU.mult).then_inc(hh, 1))
                        prd_done[(th, c)] = kb.val("vec")
                ee = "v"
                pw2 = list(wguard)
                if first_w2[0]:
                    pw2.append(("sca", S_H1))
                    first_w2[0] = False
                mm_group(1, DFQC,
                         lambda k, cc, wv=wv: wv(k, 0, P),
                         lambda k: v16(H1 + k * NMM, NMM),
                         evac2, ee, pre_wait=pw2)
                slot_done(si)
        V_PRODALL = kb.val("vec")
        for j in range(NTH):
            J = th * NTH + j
            kb.wait("y", "xr", kb.val("xr"))
            if first_xr[0]:
                kb.wait("y", "vec", V_REND)
                first_xr[0] = False
            if J >= 2 and (J - 2) in out_done:
                kb.wait("y", "vec", out_done[J - 2])
            h, _ = kb.inc("xr", 16)
            kb.y(lambda e, J=J, hh=h: e.dma_start(
                out=v32(FXJ + (J % 2) * DM, DM),
                in_=dx[J * P:(J + 1) * P, :]).then_inc(hh, 16))
            XRV = kb.val("xr")
            kb.wait("v", "xr", XRV)
            if (J - 2) in out_dma_val:
                kb.wait("v", "outw", out_dma_val[J - 2])
            for c in range(DMC):
                tpi = n_ftp["n"]
                n_ftp["n"] += 1
                slot = FTP_SLOTS[tpi % 2]
                cast = v32(FTP + (tpi % 2) * P, P)
                if tpi >= 2:
                    kb.wait("v", "ten", tp_ten[tpi - 2])
                kb.wait("v", "vec", V_PRODALL)
                h, _ = kb.inc("vec")
                kb.v(lambda e, cst=cast, c=c, j=j, hh=h: e.tensor_copy(
                    out=cst, in_=v16(HS + c * NMM + j * P, P)).then_inc(hh, 1))
                kb.wait("t", "vec", kb.val("vec"))
                if tpi >= 2:
                    kb.wait("t", "vec", tp_evac[tpi - 2])
                h, _ = kb.inc("ten")
                kb.t(lambda e, sl=slot, cst=cast, hh=h: e.transpose(
                    sl, cst, cidf[:, :]).then_inc(hh, 1))
                if tpi == 0:
                    tp_ten, tp_evac = {}, {}
                tp_ten[tpi] = kb.val("ten")
                kb.wait("v", "ten", tp_ten[tpi])
                h, _ = kb.inc("vec")
                kb.v(lambda e, sl=slot, c=c, J=J, hh=h:
                     e.scalar_tensor_tensor(
                         out=v32(FOJ + (J % 2) * DM + c * P, P), in0=sl,
                         scalar=v32(MSK + J, 1)[:, 0:1],
                         in1=v32(FXJ + (J % 2) * DM + c * P, P),
                         op0=ALU.mult, op1=ALU.add).then_inc(hh, 1))
                tp_evac[tpi] = kb.val("vec")
                drip_piece(1)
            out_done[J] = kb.val("vec")
            kb.wait("y", "vec", out_done[J])
            h, _ = kb.inc("outw", 16)
            kb.y(lambda e, J=J, hh=h: e.dma_start(
                out=dout[J * P:(J + 1) * P, :],
                in_=v32(FOJ + (J % 2) * DM, DM)).then_inc(hh, 16))
            out_dma_val[J] = kb.val("outw")

    if cfg.get("debug"):
        dbg_xn = nc.dram_tensor("dbg_xn", [P, NTC * DM], BF16,
                                kind="ExternalOutput")
        dbg_h = nc.dram_tensor("dbg_h", [P, HIDC * TOKC], BF16,
                               kind="ExternalOutput")
        dbg_s = nc.dram_tensor("dbg_s", [P, DMC * TOKC], BF16,
                               kind="ExternalOutput")
        dbg_t = nc.dram_tensor("dbg_tail", [P, 224], F32,
                               kind="ExternalOutput")
        dbg_fo = nc.dram_tensor("dbg_fo", [P, DMC * NMM], F32,
                                kind="ExternalOutput")
        dbg_xp = nc.dram_tensor("dbg_xp", [P, HIDC * TOKC], BF16,
                                kind="ExternalOutput")
        kb.wait("y", "outw", kb.val("outw"))
        for dst, srcap in ((dbg_xn.ap(), xn_hbm.ap()),
                           (dbg_h.ap(), h_hbm.ap()),
                           (dbg_s.ap(), s_hbm.ap()),
                           (dbg_t.ap(), v32(PT, 224)),
                           (dbg_fo.ap(), v32(FO, DMC * NMM)),
                           (dbg_xp.ap(), v16(XP, HIDC * TOKC))):
            h, _ = kb.inc("outw", 16)
            kb.y(lambda e, d=dst, s2=srcap, hh=h:
                 e.dma_start(out=d, in_=s2).then_inc(hh, 16))
    kb.wait("y", "outw", kb.val("outw"))
    return kb.emit()


def _layouts(inputs, cfg):
    B, SEQ, DM, HID, HEADS, DF = (cfg[k] for k in
                                  ("B", "SEQ", "DM", "HID", "HEADS", "DF"))
    TOKC = SEQ // 4
    DMC, HIDC, DFC = DM // P, HID // P, DF // P
    HD = HID // HEADS
    HDC = HD // P
    GC = 4 * HID // P
    f32 = np.float32
    x = np.asarray(inputs["L0_attn_x"], f32)
    shared = dict(
        winT=np.ascontiguousarray(np.asarray(inputs["L0_attn_W_in"], f32).T),
        wgT=np.ascontiguousarray(np.asarray(inputs["L0_attn_W_gate"], f32).T),
        woutT=np.ascontiguousarray(np.asarray(inputs["L0_attn_W_out"], f32).T),
        w1=np.ascontiguousarray(np.asarray(inputs["L0_ffn_w1"], f32)),
        w2=np.ascontiguousarray(np.asarray(inputs["L0_ffn_w2"], f32)),
        bg=np.ascontiguousarray(
            np.asarray(inputs["L0_attn_b_gate"], f32).reshape(GC, P).T),
        b1c=np.ascontiguousarray(
            np.asarray(inputs["L0_ffn_b1"], f32).reshape(DFC, P).T),
        b2c=np.ascontiguousarray(
            np.asarray(inputs["L0_ffn_b2"], f32).reshape(DMC, P).T),
        rmsc=np.ascontiguousarray(
            np.asarray(inputs["L0_attn_rms_w"], f32).reshape(HDC, P).T),
        wrb=np.ascontiguousarray(
            np.broadcast_to(np.asarray(inputs["L0_ffn_w_r"], f32), (P, DM))),
        brb=np.full((P, 1), np.asarray(inputs["L0_ffn_b_r"], f32), f32),
        identf=np.eye(P, dtype=f32),
        onespp=np.ones((P, P), f32),
        epsc=np.broadcast_to(np.asarray(
            [cfg["LN_EPS"], cfg["RMS_EPS"]], f32), (P, 2)).copy(),
    )
    in_maps = []
    for core in range(8):
        b, q = core // 4, core % 4
        mq = np.zeros((P, 3), f32)
        if q >= 1:
            mq[:, q - 1] = 1.0
        m = dict(shared)
        m["x_own"] = np.ascontiguousarray(x[b, q * TOKC:(q + 1) * TOKC, :])
        m["mq"] = mq
        in_maps.append(m)
    return in_maps


def kernel(**inputs):
    cfg = FULL_CFG
    in_maps = _layouts(inputs, cfg)
    nc = build(cfg)
    res = run_bass_kernel_spmd(nc, in_maps, core_ids=list(range(8)))
    B, SEQ, DM = cfg["B"], cfg["SEQ"], cfg["DM"]
    TOKC = SEQ // 4
    out = np.empty((B, SEQ, DM), np.float32)
    for core in range(8):
        b, q = core // 4, core % 4
        out[b, q * TOKC:(q + 1) * TOKC, :] = res.results[core]["out"]
    return out
